# version 04
# baseline (speedup 1.0000x reference)
import contextlib
import numpy as np
import ml_dtypes
import concourse.bass as bass
import concourse.mybir as mybir
from concourse.bass_utils import run_bass_kernel_spmd

F32 = mybir.dt.float32
BF16 = mybir.dt.bfloat16
U8 = mybir.dt.uint8
AF = mybir.ActivationFunctionType
ALU = mybir.AluOpType
AX = mybir.AxisListType

B = 4
T = 2048
C = 1024
DH = 4096
NE = 8
NCC = C // 128
NHC = DH // 128
TT = 512
NTILE_SEQ = T // TT
NTILE = B * NTILE_SEQ
NTOK = B * T
NQT = T // 128
NEG = -1.0e30
EPS = 1e-5
SCALE = 1.0 / 32.0


def nkb(qt):
    return qt // 4 + 1


_PB_PRE = [0]
for _q in range(NQT):
    _PB_PRE.append(_PB_PRE[-1] + nkb(_q))
TR_PER_SEQ = 16 + _PB_PRE[NQT]


def trA(b, tt, s):
    return TR_PER_SEQ * b + tt * 4 + s


def trA_sub(sg):
    return TR_PER_SEQ * (sg // 16) + (sg % 16)


def trB(b, qt, kb):
    return TR_PER_SEQ * b + 16 + _PB_PRE[qt] + kb


def sgi(b, qt, kb):
    return _PB_PRE[NQT] * b + _PB_PRE[qt] + kb


def build_nc(debug=False):
    Y0M = 2 if debug else 1
    nc = bass.Bass("TRN2", target_bir_lowering=False)
    x_in = nc.dram_tensor("x", [B, T, C], F32, kind="ExternalInput").ap()
    wq_in = nc.dram_tensor("wq", [C, C], F32, kind="ExternalInput").ap()
    wk_in = nc.dram_tensor("wk", [C, C], F32, kind="ExternalInput").ap()
    wv_in = nc.dram_tensor("wv", [C, C], F32, kind="ExternalInput").ap()
    wg_in = nc.dram_tensor("wg", [C, NE], F32, kind="ExternalInput").ap()
    bgb_in = nc.dram_tensor("bgb", [128, 4 * NE], F32, kind="ExternalInput").ap()
    w1_in = nc.dram_tensor("w1", [NE, C, DH], F32, kind="ExternalInput").ap()
    b1T_in = nc.dram_tensor("b1T", [NE, 128, NHC], F32, kind="ExternalInput").ap()
    w2_in = nc.dram_tensor("w2", [NE, DH, C], F32, kind="ExternalInput").ap()
    b2_in = nc.dram_tensor("b2", [NE, C], F32, kind="ExternalInput").ap()
    ident_in = nc.dram_tensor("ident", [128, 128], BF16, kind="ExternalInput").ap()
    mask_in = nc.dram_tensor("mask", [128, 4 * 512], F32, kind="ExternalInput").ap()
    y_out = nc.dram_tensor("y", [B, T, C], F32, kind="ExternalOutput").ap()
    xnT = nc.dram_tensor("xnT", [C, NTOK], BF16, kind="Internal").ap()
    if debug:
        xnT_dbg = nc.dram_tensor("xnT_dbg", [C, NTOK], BF16, kind="ExternalOutput").ap()
        g_dbg = nc.dram_tensor("g_dbg", [128, NTILE * 4 * NE], F32, kind="ExternalOutput").ap()
        y0_dbg = nc.dram_tensor("y0_dbg", [B, T, C], F32, kind="ExternalOutput").ap()

    es = contextlib.ExitStack()
    with es:
        E = es.enter_context
        sb_g = E(nc.sbuf_tensor("sb_g", [128, NTILE * 4 * NE], F32))
        sb_b1 = E(nc.sbuf_tensor("sb_b1", [128, NE * NHC], F32))
        sb_id = E(nc.sbuf_tensor("sb_id", [128, 128], BF16))
        ones = E(nc.sbuf_tensor("ones", [1, 128], BF16))
        pool = E(nc.psum_tensor("pool", [128, 4096], F32))
        _sn = ["sm_pro", "sm_x", "sm_sq", "sm_veps", "sm_sd", "sm_xnt",
               "sm_tr", "sm_trc", "sm_mm", "sm_qe", "sm_gp", "sm_gev",
               "sm_xnfs", "sm_wqkv", "sm_s", "sm_max", "sm_exp", "sm_av",
               "sm_stt", "sm_xb", "sm_y0", "sm_ab", "sm_pc", "sm_w",
               "sm_xn", "sm_h", "sm_silu", "sm_o", "sm_y", "sm_acc", "sm_dve", "sm_sqr", "sm_junk"]
        (sm_pro, sm_x, sm_sq, sm_veps, sm_sd, sm_xnt, sm_tr, sm_trc, sm_mm,
         sm_qe, sm_gp, sm_gev, sm_xnfs, sm_wqkv, sm_s, sm_max, sm_exp, sm_av,
         sm_stt, sm_xb, sm_y0, sm_ab, sm_pc, sm_w, sm_xn, sm_h, sm_silu,
         sm_o, sm_y, sm_acc, sm_dve, sm_sqr, sm_junk) = (E(nc.semaphore(n)) for n in _sn)
        blk = E(nc.Block())
        ps_qkv = pool[:, 0:1024]
        ps_v = pool[:, 1024:2048]
        ps_gate = pool[:, 2048:2080]
        ps_s = pool[:, 0:2048]
        ps_av = pool[:, 2048:3072]
        ps_tr = pool[:, 3072:4096].bitcast(BF16)
        ps_h = pool[:, 0:1024]
        ps_o = pool[:, 2048:4096]

        es_ab = contextlib.ExitStack()
        with es_ab:
            EA = es_ab.enter_context
            arena = EA(nc.sbuf_tensor("arena", [128, 49152], U8))
            sb_q = EA(nc.sbuf_tensor("sb_q", [128, NCC * T], BF16))
            sb_k = EA(nc.sbuf_tensor("sb_k", [128, NCC * T], BF16))
            sb_v = EA(nc.sbuf_tensor("sb_v", [128, NQT * C], BF16))
            sb_xnf = EA(nc.sbuf_tensor("sb_xnf", [128, 2 * NCC * TT], BF16))
            sb_xntk = EA(nc.sbuf_tensor("sb_xntk", [128, 2 * C], BF16))
            sb_xa = EA(nc.sbuf_tensor("sb_xa", [128, 5 * C], F32))
            sb_xb2 = EA(nc.sbuf_tensor("sb_xb2", [128, 2 * C], F32))
            sb_y0 = EA(nc.sbuf_tensor("sb_y0", [128, 2 * C], F32))
            sb_wg = EA(nc.sbuf_tensor("sb_wg", [128, NCC * NE], BF16))
            sb_bgb = EA(nc.sbuf_tensor("sb_bgb", [128, 4 * NE], F32))
            sb_st = EA(nc.sbuf_tensor("sb_st", [128, 32], F32))
            sb_gs = EA(nc.sbuf_tensor("sb_gs", [128, 64], F32))
            sb_rs = EA(nc.sbuf_tensor("sb_rs", [128, 8], F32))
            sb_sd = EA(nc.sbuf_tensor("sb_sd", [128, 20], F32))
            wqkv_a = [arena[:, i * 16384:(i + 1) * 16384].bitcast(BF16) for i in range(3)]
            sb_p = arena[:, 0:8192].bitcast(BF16)
            sb_pT = arena[:, 8192:12288].bitcast(BF16)
            sb_msk = arena[:, 12288:20480].bitcast(F32)
            sqs = sb_xb2.bitcast(F32)

            @blk.vector
            def _(e):
                e.memset(ones[:], 1.0).then_inc(sm_pro, 1)

            @blk.sync
            def _(e):
                e.dma_start(out=sb_id[:], in_=ident_in).then_inc(sm_pro, 16)
                e.dma_start(out=sb_bgb[:], in_=bgb_in).then_inc(sm_pro, 16)
                e.dma_start(out=sb_b1[:].rearrange("p (e k) -> p e k", e=NE),
                            in_=b1T_in.rearrange("e p k -> p e k")).then_inc(sm_pro, 16)

            @blk.gpsimd
            def _(e):
                e.dma_start(out=sb_wg[:].rearrange("p (g n) -> p g n", g=NCC),
                            in_=wg_in.rearrange("(g p) n -> p g n", p=128)).then_inc(sm_wqkv, 16)
                for wi, w_in in enumerate([wq_in, wk_in, wv_in]):
                    e.dma_start(out=wqkv_a[wi].rearrange("p (g h) -> p g h", g=NCC),
                                in_=w_in.rearrange("(g p) h -> p g h", p=128)).then_inc(sm_wqkv, 16)
                for b in range(1, B):
                    e.wait_ge(sm_av, NQT * b)
                    e.wait_ge(sm_stt, NQT * b)
                    e.wait_ge(sm_exp, NQT * b)
                    e.wait_ge(sm_y0, Y0M * 16 * NQT * b)
                    for wi, w_in in enumerate([wq_in, wk_in, wv_in]):
                        e.dma_start(out=wqkv_a[wi].rearrange("p (g h) -> p g h", g=NCC),
                                    in_=w_in.rearrange("(g p) h -> p g h", p=128)).then_inc(sm_wqkv, 16)

            @blk.sync
            def _(e):
                for b in range(B):
                    for tt in range(NTILE_SEQ):
                        ti = b * NTILE_SEQ + tt
                        for s in range(4):
                            sg = ti * 4 + s
                            sl = sg % 5
                            if sg >= 5:
                                e.wait_ge(sm_xnt, sg - 5 + 1)
                            e.dma_start(out=sb_xa[:, sl * C:(sl + 1) * C],
                                        in_=x_in[b, tt * TT + s * 128: tt * TT + (s + 1) * 128, :]
                                        ).then_inc(sm_junk, 16)
                            e.dma_start(out=sb_xa[:, (sl + 1) * C - 16:(sl + 1) * C],
                                        in_=x_in[b, tt * TT + s * 128: tt * TT + (s + 1) * 128, C - 16:C]
                                        ).then_inc(sm_x, 16)
                        e.wait_ge(sm_trc, trA(b, tt, 3) + 1)
                        tok0 = b * T + tt * TT
                        e.dma_start(
                            out=xnT[:, tok0:tok0 + TT].rearrange("(g p) t -> p g t", p=128),
                            in_=sb_xnf[:, (ti % 2) * NCC * TT:(ti % 2 + 1) * NCC * TT]
                                .rearrange("p (g t) -> p g t", g=NCC)).then_inc(sm_xnfs, 16)
                    e.wait_ge(sm_gp, 4 * (b + 1))
                    e.dma_start(out=sb_msk, in_=mask_in).then_inc(sm_xb, 16)
                    for qt in range(NQT):
                        qg = NQT * b + qt
                        if qg >= 2:
                            e.wait_ge(sm_stt, qg - 1)
                        e.dma_start(out=sb_xb2[:, (qg % 2) * C:(qg % 2 + 1) * C],
                                    in_=x_in[b, qt * 128:(qt + 1) * 128, :]).then_inc(sm_xb, 16)
                        e.wait_ge(sm_stt, qg + 1)
                        e.dma_start(out=y_out[b, qt * 128:(qt + 1) * 128, :],
                                    in_=sb_y0[:, (qg % 2) * C:(qg % 2 + 1) * C]).then_inc(sm_y0, 16)
                        if debug:
                            e.dma_start(out=y0_dbg[b, qt * 128:(qt + 1) * 128, :],
                                        in_=sb_y0[:, (qg % 2) * C:(qg % 2 + 1) * C]).then_inc(sm_y0, 16)

            @blk.scalar
            def _(e):
                e.wait_ge(sm_pro, 33)
                for b in range(B):
                    if b > 0:
                        e.wait_ge(sm_av, NQT * b)
                    for tt in range(NTILE_SEQ):
                        ti = b * NTILE_SEQ + tt
                        e.wait_ge(sm_veps, ti + 1)
                        if b > 0 and tt == 0:
                            e.wait_ge(sm_stt, NQT * b)
                        e.activation(sb_sd[:], sb_st[:, 0:20], AF.Sqrt)
                        e.activation(sb_sd[:, 16:20], sb_st[:, 0:4], AF.Square).then_inc(sm_sd, 1)
                        for j in range(20):
                            e.wait_ge(sm_mm, 20 * ti + j + 1)
                            if j < 8:
                                e.activation(sb_q[:, j * T + tt * TT: j * T + (tt + 1) * TT],
                                             ps_qkv[:, (j % 2) * 512:(j % 2 + 1) * 512],
                                             AF.Copy).then_inc(sm_qe, 1)
                            elif j < 16:
                                jj = j - 8
                                e.activation(sb_k[:, jj * T + tt * TT: jj * T + (tt + 1) * TT],
                                             ps_qkv[:, (j % 2) * 512:(j % 2 + 1) * 512],
                                             AF.Copy).then_inc(sm_qe, 1)
                            else:
                                s = j - 16
                                e.activation(sb_v[:, (tt * 4 + s) * C:(tt * 4 + s + 1) * C],
                                             ps_v[:], AF.Copy).then_inc(sm_qe, 1)
                        e.wait_ge(sm_gev, 6 * ti + 2)
                        e.activation(sb_gs[:, 32:64], sb_gs[:, 0:32], AF.Exp)
                        e.activation(sb_sd[:, 16:20], sb_st[:, 0:4], AF.Square).then_inc(sm_qe, 4)
                    for qt in range(NQT):
                        qg = NQT * b + qt
                        n = nkb(qt)
                        e.wait_ge(sm_max, qg + 1)
                        if qg >= 2:
                            e.wait_ge(sm_tr, trB((qg - 2) // NQT, (qg - 2) % NQT, nkb((qg - 2) % NQT) - 1) + 1)
                        e.activation(sb_p[:, (qg % 2) * 2048:(qg % 2) * 2048 + n * 512],
                                     ps_s[:, 0:n * 512], AF.Exp,
                                     scale=SCALE, bias=sb_rs[:, 0:1])
                        e.activation(sb_sd[:, 16:20], sb_st[:, 0:4], AF.Square).then_inc(sm_exp, 1)

            @blk.vector
            def _(e):
                dv = [0]

                def mark(op):
                    op.then_inc(sm_dve, 1)
                    dv[0] += 1

                def sync():
                    e.wait_ge(sm_dve, dv[0])

                for b in range(B):
                    for tt in range(NTILE_SEQ):
                        ti = b * NTILE_SEQ + tt
                        for s in range(4):
                            sg = ti * 4 + s
                            e.wait_ge(sm_x, 16 * (sg + 1))
                            if b > 0 and tt == 0 and s == 0:
                                e.wait_ge(sm_stt, NQT * b)
                            sl = sg % 5
                            e.reduce_sum(sb_st[:, s:s + 1], sb_xa[:, sl * C:(sl + 1) * C], axis=AX.X)
                            mark(e.tensor_mul(sqs[:, (s % 2) * C:(s % 2 + 1) * C],
                                              sb_xa[:, sl * C:(sl + 1) * C],
                                              sb_xa[:, sl * C:(sl + 1) * C]))
                            sync()
                            r = e.reduce_sum(sb_st[:, 8 + s:9 + s], sqs[:, (s % 2) * C:(s % 2 + 1) * C],
                                             axis=AX.X)
                            if s == 3:
                                mark(r)
                        sync()
                        if ti > 0:
                            e.wait_ge(sm_xnt, 4 * ti)
                        mark(e.tensor_scalar_mul(sb_st[:, 4:8], sb_st[:, 0:4], -1.0 / C))
                        sync()
                        mark(e.tensor_scalar(sb_st[:, 12:16], sb_st[:, 8:12], 1.0 / C, EPS,
                                             ALU.mult, ALU.add))
                        mark(e.tensor_mul(sb_st[:, 20:24], sb_st[:, 4:8], sb_st[:, 4:8]))
                        sync()
                        e.tensor_sub(sb_st[:, 12:16], sb_st[:, 12:16], sb_st[:, 20:24]).then_inc(sm_veps, 1)
                        e.wait_ge(sm_sd, ti + 1)
                        mark(e.reciprocal(sb_st[:, 20:24], sb_sd[:, 12:16]))
                        sync()
                        for s in range(4):
                            sg = ti * 4 + s
                            if sg >= 2:
                                e.wait_ge(sm_tr, trA_sub(sg - 2) + 1)
                            e.tensor_scalar(sb_xntk[:, (s % 2) * C:(s % 2 + 1) * C],
                                            sb_xa[:, (sg % 5) * C:(sg % 5 + 1) * C],
                                            sb_st[:, 4 + s:5 + s], sb_st[:, 20 + s:21 + s],
                                            ALU.add, ALU.mult).then_inc(sm_xnt, 1)
                            g = trA(b, tt, s)
                            e.wait_ge(sm_tr, g + 1)
                            if s == 0 and ti >= 2:
                                e.wait_ge(sm_gp, ti - 1)
                                e.wait_ge(sm_xnfs, 16 * (ti - 1))
                            e.tensor_copy(
                                sb_xnf[:, (ti % 2) * NCC * TT:(ti % 2 + 1) * NCC * TT]
                                .rearrange("p (g t) -> p g t", g=NCC)[:, :, s * 128:(s + 1) * 128],
                                ps_tr[:, (g % 2) * 1024:(g % 2) * 1024 + 1024]
                                .rearrange("p (g t) -> p g t", g=NCC)).then_inc(sm_trc, 1)
                        e.wait_ge(sm_gp, ti + 1)
                        if ti > 0:
                            e.wait_ge(sm_gev, 6 * ti)
                        e.tensor_copy(sb_gs[:, 0:32], ps_gate[:]).then_inc(sm_gev, 1)
                        e.wait_ge(sm_gev, 6 * ti + 1)
                        e.tensor_add(sb_gs[:, 0:32], sb_gs[:, 0:32], sb_bgb[:]).then_inc(sm_gev, 1)
                        for s in range(4):
                            e.wait_ge(sm_qe, 24 * ti + 20 + s + 1)
                            mark(e.reduce_sum(sb_st[:, 28 + s:29 + s],
                                              sb_gs[:, 32 + s * 8:32 + (s + 1) * 8], axis=AX.X))
                            sync()
                            e.reciprocal(sb_st[:, 24 + s:25 + s], sb_st[:, 28 + s:29 + s])
                            mark(e.reduce_max(sb_st[:, 28 + s:29 + s],
                                              sb_gs[:, 32 + s * 8:32 + (s + 1) * 8], axis=AX.X))
                            sync()
                            mark(e.tensor_scalar(sb_gs[:, s * 8:(s + 1) * 8],
                                                 sb_gs[:, 32 + s * 8:32 + (s + 1) * 8],
                                                 sb_st[:, 28 + s:29 + s], None, ALU.is_ge))
                            sync()
                            e.scalar_tensor_tensor(sb_g[:, (ti * 4 + s) * 8:(ti * 4 + s + 1) * 8],
                                                   sb_gs[:, 32 + s * 8:32 + (s + 1) * 8],
                                                   sb_st[:, 24 + s:25 + s],
                                                   sb_gs[:, s * 8:(s + 1) * 8],
                                                   ALU.mult, ALU.mult).then_inc(sm_gev, 1)
                    for qt in range(NQT):
                        qg = NQT * b + qt
                        n = nkb(qt)
                        if qt == 0:
                            e.wait_ge(sm_xb, 16 * (NQT * b + b + 1))
                        e.wait_ge(sm_s, sgi(b, qt, n - 1) + 1)
                        mark(e.tensor_add(ps_s[:, (n - 1) * 512:n * 512],
                                          ps_s[:, (n - 1) * 512:n * 512],
                                          sb_msk[:, (qt % 4) * 512:(qt % 4 + 1) * 512]))
                        sync()
                        mark(e.reduce_max(sb_rs[:, 0:1], ps_s[:, 0:n * 512], axis=AX.X, negate=True))
                        sync()
                        e.tensor_scalar_mul(sb_rs[:, 0:1], sb_rs[:, 0:1], SCALE).then_inc(sm_max, 1)
                        for kb in range(n):
                            g = trB(b, qt, kb)
                            e.wait_ge(sm_tr, g + 1)
                            if kb == 0 and qg >= 1:
                                e.wait_ge(sm_av, qg)
                            e.tensor_copy(sb_pT[:, kb * 512:(kb + 1) * 512],
                                          ps_tr[:, (g % 2) * 1024:(g % 2 + 1) * 1024][:, 0:512]
                                          ).then_inc(sm_trc, 1)
                        e.wait_ge(sm_exp, qg + 1)
                        mark(e.reduce_sum(sb_rs[:, 1:2],
                                          sb_p[:, (qg % 2) * 2048:(qg % 2) * 2048 + n * 512],
                                          axis=AX.X))
                        sync()
                        mark(e.reciprocal(sb_rs[:, 2:3], sb_rs[:, 1:2]))
                        sync()
                        e.wait_ge(sm_av, qg + 1)
                        e.wait_ge(sm_xb, 16 * (qg + 1) + 16 * (b + 1))
                        if qg >= 2:
                            e.wait_ge(sm_y0, Y0M * 16 * (qg - 1))
                        e.scalar_tensor_tensor(sb_y0[:, (qg % 2) * C:(qg % 2 + 1) * C],
                                               ps_av[:], sb_rs[:, 2:3],
                                               sb_xb2[:, (qg % 2) * C:(qg % 2 + 1) * C],
                                               ALU.mult, ALU.add).then_inc(sm_stt, 1)

            @blk.tensor
            def _(e):
                e.wait_ge(sm_pro, 33)
                for b in range(B):
                    for tt in range(NTILE_SEQ):
                        ti = b * NTILE_SEQ + tt
                        for s in range(4):
                            g = trA(b, tt, s)
                            e.wait_ge(sm_xnt, ti * 4 + s + 1)
                            if g >= 2:
                                e.wait_ge(sm_trc, g - 1)
                            for c in range(NCC):
                                tr = e.transpose(
                                    ps_tr[:, (g % 2) * 1024 + c * 128:(g % 2) * 1024 + (c + 1) * 128],
                                    sb_xntk[:, (s % 2) * C + c * 128:(s % 2) * C + (c + 1) * 128],
                                    sb_id[:])
                                if c == NCC - 1:
                                    tr.then_inc(sm_tr, 1)
                        e.wait_ge(sm_trc, trA(b, tt, 3) + 1)
                        if b == 0 and tt == 0:
                            e.wait_ge(sm_wqkv, 64)
                        if b > 0 and tt == 0:
                            e.wait_ge(sm_wqkv, 64 + 48 * b)
                            e.wait_ge(sm_exp, NQT * b)
                        xf = sb_xnf[:, (ti % 2) * NCC * TT:(ti % 2 + 1) * NCC * TT]
                        jj = 0
                        for wi, wbuf in enumerate([wqkv_a[0], wqkv_a[1]]):
                            for m in range(8):
                                j = jj
                                jj += 1
                                e.wait_ge(sm_qe, 24 * ti + j - 1) if (24 * ti + j) >= 2 else None
                                pb = ps_qkv[:, (j % 2) * 512:(j % 2 + 1) * 512]
                                for c in range(NCC):
                                    mm = e.matmul(pb, wbuf[:, c * C + m * 128:c * C + (m + 1) * 128],
                                                  xf[:, c * TT:(c + 1) * TT],
                                                  start=(c == 0), stop=(c == NCC - 1))
                                    if c == NCC - 1:
                                        mm.then_inc(sm_mm, 1)
                        for s in range(4):
                            j = 16 + s
                            e.wait_ge(sm_qe, 24 * ti + j - 1 + 1) if (ti > 0 or s > 0) else None
                            for c in range(NCC):
                                for hh in range(2):
                                    mm = e.matmul(ps_v[:, hh * 512:(hh + 1) * 512],
                                                  xf[:, c * TT + s * 128:c * TT + (s + 1) * 128],
                                                  wqkv_a[2][:, c * C + hh * 512:c * C + (hh + 1) * 512],
                                                  start=(c == 0), stop=(c == NCC - 1))
                                    if c == NCC - 1 and hh == 1:
                                        mm.then_inc(sm_mm, 1)
                        if ti > 0:
                            e.wait_ge(sm_gev, 6 * (ti - 1) + 1)
                        if b > 0 and tt == 0:
                            e.wait_ge(sm_stt, NQT * b)
                        for s in range(4):
                            for c in range(NCC):
                                mm = e.matmul(ps_gate[:, s * 8:(s + 1) * 8],
                                              xf[:, c * TT + s * 128:c * TT + (s + 1) * 128],
                                              sb_wg[:, c * 8:(c + 1) * 8],
                                              start=(s == 0 and c == 0), stop=(c == NCC - 1))
                                if s == 3 and c == NCC - 1:
                                    mm.then_inc(sm_gp, 1)
                    for qt in range(NQT):
                        qg = NQT * b + qt
                        n = nkb(qt)
                        for kb in range(n):
                            if qg > 0 and kb == 0:
                                e.wait_ge(sm_exp, qg)
                            if qt == 0 and kb == 0:
                                e.wait_ge(sm_qe, 96 * (b + 1))
                            for c in range(8):
                                mm = e.matmul(ps_s[:, kb * 512:(kb + 1) * 512],
                                              sb_q[:, c * T + qt * 128:c * T + (qt + 1) * 128],
                                              sb_k[:, c * T + kb * 512:c * T + (kb + 1) * 512],
                                              start=(c == 0), stop=(c == 7))
                                if c == 7:
                                    mm.then_inc(sm_s, 1)
                        e.wait_ge(sm_exp, qg + 1)
                        for kb in range(n):
                            g = trB(b, qt, kb)
                            if g >= 2:
                                e.wait_ge(sm_trc, g - 1)
                            for u in range(4):
                                tr = e.transpose(
                                    ps_tr[:, (g % 2) * 1024 + u * 128:(g % 2) * 1024 + (u + 1) * 128],
                                    sb_p[:, (qg % 2) * 2048 + kb * 512 + u * 128:
                                         (qg % 2) * 2048 + kb * 512 + (u + 1) * 128],
                                    sb_id[:])
                                if u == 3:
                                    tr.then_inc(sm_tr, 1)
                        e.wait_ge(sm_trc, trB(b, qt, n - 1) + 1)
                        if qg > 0:
                            e.wait_ge(sm_stt, qg)
                        if qt == 0:
                            e.wait_ge(sm_gev, 6 * (4 * b + 3) + 1)
                        for sc in range(n * 4):
                            for hh in range(2):
                                mm = e.matmul(ps_av[:, hh * 512:(hh + 1) * 512],
                                              sb_pT[:, sc * 128:(sc + 1) * 128],
                                              sb_v[:, sc * C + hh * 512:sc * C + (hh + 1) * 512],
                                              start=(sc == 0), stop=(sc == n * 4 - 1))
                                if sc == n * 4 - 1 and hh == 1:
                                    mm.then_inc(sm_av, 1)

        es_c = contextlib.ExitStack()
        with es_c:
            EC = es_c.enter_context
            sb_w1 = EC(nc.sbuf_tensor("sb_w1", [128, NCC * DH], BF16))
            sb_w2 = EC(nc.sbuf_tensor("sb_w2", [128, NHC * C], BF16))
            sb_xnc = EC(nc.sbuf_tensor("sb_xnc", [128, 2 * NCC * TT], BF16))
            sb_h = EC(nc.sbuf_tensor("sb_h", [128, NHC * TT], BF16))
            sb_y = EC(nc.sbuf_tensor("sb_y", [128, 2 * 2 * C], F32))
            sb_b2 = EC(nc.sbuf_tensor("sb_b2", [1, C], BF16))
            @blk.sync
            def _(e):
                e.wait_ge(sm_exp, NQT * B)
                e.wait_ge(sm_stt, NQT * B)
                e.wait_ge(sm_av, NQT * B)
                e.wait_ge(sm_y0, Y0M * 16 * NQT * B)
                for ex in range(NE):
                    for tt in range(NTILE):
                        i = ex * NTILE + tt
                        if i >= 2:
                            e.wait_ge(sm_h, NHC * (i - 1))
                        tok0 = tt * TT
                        src = xnT[:, tok0:tok0 + TT].rearrange("(g p) t -> p g t", p=128)
                        dst = sb_xnc[:, (i % 2) * NCC * TT:(i % 2 + 1) * NCC * TT]
                        e.dma_start(out=dst.rearrange("p (g t) -> p g t", g=NCC),
                                    in_=src).then_inc(sm_xn, 16)
                if debug:
                    e.wait_ge(sm_xnfs, 16 * NTILE)
                    e.dma_start(out=xnT_dbg, in_=xnT).then_inc(sm_xn, 16)
                    e.dma_start(out=g_dbg, in_=sb_g[:]).then_inc(sm_xn, 16)

            @blk.tensor
            def _(e):
                for ex in range(NE):
                    for tt in range(NTILE):
                        i = ex * NTILE + tt
                        e.wait_ge(sm_xn, 16 * (i + 1))
                        xt = sb_xnc[:, (i % 2) * NCC * TT:(i % 2 + 1) * NCC * TT]
                        for k in range(NHC):
                            gi = i * NHC + k
                            if tt == 0 and k == 0:
                                e.wait_ge(sm_w, 80 * ex + 16)
                            if tt == 0 and k == NHC // 2:
                                e.wait_ge(sm_w, 80 * ex + 32)
                            if gi >= 2:
                                e.wait_ge(sm_silu, gi - 1)
                            pb = ps_h[:, (gi % 2) * TT:(gi % 2 + 1) * TT]
                            for c in range(NCC):
                                mm = e.matmul(pb, sb_w1[:, (c * DH + k * 128):(c * DH + (k + 1) * 128)],
                                              xt[:, c * TT:(c + 1) * TT],
                                              start=(c == 0), stop=(c == NCC - 1))
                                if c == NCC - 1:
                                    mm.then_inc(sm_h, 1)
                        if tt == 0:
                            e.wait_ge(sm_w, 80 * (ex + 1))
                        e.wait_ge(sm_silu, (i + 1) * NHC)
                        for s in range(4):
                            oi = i * 4 + s
                            if oi >= 2:
                                e.wait_ge(sm_y, oi - 1)
                            po = ps_o[:, (oi % 2) * C:(oi % 2 + 1) * C]
                            for k in range(NHC):
                                hT = sb_h[:, (k * TT + s * 128):(k * TT + (s + 1) * 128)]
                                e.matmul(po[:, 0:512], hT, sb_w2[:, k * C:k * C + 512],
                                         start=(k == 0), stop=False)
                                e.matmul(po[:, 512:1024], hT, sb_w2[:, k * C + 512:(k + 1) * C],
                                         start=(k == 0), stop=False)
                            e.matmul(po[:, 0:512], ones[0:1, 0:128], sb_b2[0:1, 0:512],
                                     start=False, stop=True)
                            e.matmul(po[:, 512:1024], ones[0:1, 0:128], sb_b2[0:1, 512:1024],
                                     start=False, stop=True).then_inc(sm_o, 1)

            @blk.scalar
            def _(e):
                for ex in range(NE):
                    for tt in range(NTILE):
                        i = ex * NTILE + tt
                        for k in range(NHC):
                            gi = i * NHC + k
                            e.wait_ge(sm_h, gi + 1)
                            e.activation(sb_h[:, k * TT:(k + 1) * TT],
                                         ps_h[:, (gi % 2) * TT:(gi % 2 + 1) * TT],
                                         AF.Silu, bias=sb_b1[:, ex * NHC + k:ex * NHC + k + 1]
                                         ).then_inc(sm_silu, 1)

            @blk.vector
            def _(e):
                for ex in range(NE):
                    for tt in range(NTILE):
                        i = ex * NTILE + tt
                        for s in range(4):
                            oi = i * 4 + s
                            e.wait_ge(sm_o, oi + 1)
                            if oi >= 4:
                                e.wait_ge(sm_acc, 16 * (oi // 2 - 1))
                            gcol = sb_g[:, (tt * 4 + s) * NE + ex:(tt * 4 + s) * NE + ex + 1]
                            j = oi // 2
                            e.tensor_scalar(sb_y[:, (j % 2) * 2 * C + (s % 2) * C:
                                                 (j % 2) * 2 * C + (s % 2 + 1) * C],
                                            ps_o[:, (oi % 2) * C:(oi % 2 + 1) * C], gcol, None,
                                            ALU.mult).then_inc(sm_y, 1)

            @blk.gpsimd
            def _(e):
                def load_w(ex):
                    if ex > 0:
                        e.wait_ge(sm_h, NHC * NTILE * ex)
                    for half in range(2):
                        e.dma_start(
                            out=sb_w1[:].rearrange("p (g h) -> p g h", g=NCC)[:, :, half * 2048:(half + 1) * 2048],
                            in_=w1_in[ex].rearrange("(g p) h -> p g h", p=128)[:, :, half * 2048:(half + 1) * 2048]
                        ).then_inc(sm_w, 16)

                def load_w2(ex):
                    if ex > 0:
                        e.wait_ge(sm_o, NTILE * 4 * ex)
                    for half in range(2):
                        e.dma_start(
                            out=sb_w2[:].rearrange("p (g d) -> p g d", g=NHC)[:, half * 16:(half + 1) * 16, :],
                            in_=w2_in[ex].rearrange("(g p) d -> p g d", p=128)[:, half * 16:(half + 1) * 16, :]
                        ).then_inc(sm_w, 16)
                    e.dma_start(out=sb_b2[:], in_=b2_in[ex].unsqueeze(0)).then_inc(sm_w, 16)

                e.wait_ge(sm_exp, NQT * B)
                e.wait_ge(sm_stt, NQT * B)
                e.wait_ge(sm_av, NQT * B)
                e.wait_ge(sm_y0, Y0M * 16 * NQT * B)
                load_w(0)
                load_w2(0)
                for ex in range(NE):
                    for tt in range(NTILE):
                        for half in range(2):
                            a = (ex * NTILE + tt) * 2 + half
                            e.wait_ge(sm_y, a * 2 + 2)
                            b = tt // NTILE_SEQ
                            r0 = (tt % NTILE_SEQ) * TT + half * 256
                            dst = y_out[b, r0:r0 + 256, :]
                            e.dma_start(out=dst.rearrange("(s p) d -> p s d", p=128),
                                        in_=sb_y[:, (a % 2) * 2 * C:(a % 2 + 1) * 2 * C]
                                        .rearrange("p (s d) -> p s d", s=2),
                                        accum_op=ALU.add).then_inc(sm_acc, 16)
                        if tt == 14 and ex < NE - 1:
                            load_w(ex + 1)
                    if ex < NE - 1:
                        load_w2(ex + 1)
                e.wait_ge(sm_acc, 16 * NE * NTILE * 2)

    return nc


def _host_inputs(shard, ins):
    bf = ml_dtypes.bfloat16
    b1 = np.asarray(ins["L0_ffn_b1"], dtype=np.float32)
    m = {
        "x": np.ascontiguousarray(shard),
        "wq": np.ascontiguousarray(np.asarray(ins["L0_attn_Wq"], dtype=np.float32)),
        "wk": np.ascontiguousarray(np.asarray(ins["L0_attn_Wk"], dtype=np.float32)),
        "wv": np.ascontiguousarray(np.asarray(ins["L0_attn_Wv"], dtype=np.float32)),
        "wg": np.ascontiguousarray(np.asarray(ins["L0_ffn_Wg"], dtype=np.float32)),
        "bgb": np.ascontiguousarray(np.tile(np.asarray(ins["L0_ffn_bg"], dtype=np.float32), (128, 4))),
        "w1": np.ascontiguousarray(np.asarray(ins["L0_ffn_W1"], dtype=np.float32)),
        "b1T": np.ascontiguousarray(b1.reshape(NE, NHC, 128).transpose(0, 2, 1)),
        "w2": np.ascontiguousarray(np.asarray(ins["L0_ffn_W2"], dtype=np.float32)),
        "b2": np.ascontiguousarray(np.asarray(ins["L0_ffn_b2"], dtype=np.float32)),
        "ident": np.eye(128, dtype=bf),
    }
    p = np.arange(128)[:, None]
    col = np.arange(512)[None, :]
    mask = np.zeros((128, 4 * 512), dtype=np.float32)
    for r in range(4):
        mask[:, r * 512:(r + 1) * 512] = np.where(col <= r * 128 + p, 0.0, NEG)
    m["mask"] = mask
    return m


def kernel(**inputs):
    x = np.asarray(inputs["L0_attn_x"], dtype=np.float32)
    shards = np.split(np.ascontiguousarray(x), 8, axis=0)
    nc = build_nc()
    in_maps = [_host_inputs(s, inputs) for s in shards]
    res = run_bass_kernel_spmd(nc, in_maps, core_ids=list(range(8)))
    return np.concatenate([np.asarray(r["y"]) for r in res.results], axis=0)


# version 09
# speedup vs baseline: 1.0431x; 1.0431x over previous
import contextlib
import numpy as np
import ml_dtypes
import concourse.bass as bass
import concourse.mybir as mybir
from concourse.bass_utils import run_bass_kernel_spmd

F32 = mybir.dt.float32
BF16 = mybir.dt.bfloat16
U8 = mybir.dt.uint8
AF = mybir.ActivationFunctionType
ALU = mybir.AluOpType
AX = mybir.AxisListType

B = 4
T = 2048
C = 1024
DH = 4096
NE = 8
NCC = C // 128
NHC = DH // 128
TT = 512
NTILE_SEQ = T // TT
NTILE = B * NTILE_SEQ
NTOK = B * T
NQT = T // 128
NEG = -1.0e30
EPS = 1e-5
SCALE = 1.0 / 32.0


def nkb(qt):
    return qt // 4 + 1


_PB_PRE = [0]
for _q in range(NQT):
    _PB_PRE.append(_PB_PRE[-1] + nkb(_q))
TR_PER_SEQ = 16 + _PB_PRE[NQT]


def trA(b, tt, s):
    return TR_PER_SEQ * b + tt * 4 + s


def trA_sub(sg):
    return TR_PER_SEQ * (sg // 16) + (sg % 16)


def trB(b, qt, kb):
    return TR_PER_SEQ * b + 16 + _PB_PRE[qt] + kb


def sgi(b, qt, kb):
    return _PB_PRE[NQT] * b + _PB_PRE[qt] + kb


def build_nc(debug=False):
    Y0M = 2 if debug else 1
    nc = bass.Bass("TRN2", target_bir_lowering=False)
    x_in = nc.dram_tensor("x", [B, T, C], F32, kind="ExternalInput").ap()
    wq_in = nc.dram_tensor("wq", [C, C], F32, kind="ExternalInput").ap()
    wk_in = nc.dram_tensor("wk", [C, C], F32, kind="ExternalInput").ap()
    wv_in = nc.dram_tensor("wv", [C, C], F32, kind="ExternalInput").ap()
    wg_in = nc.dram_tensor("wg", [C, NE], F32, kind="ExternalInput").ap()
    bgb_in = nc.dram_tensor("bgb", [128, 4 * NE], F32, kind="ExternalInput").ap()
    w1_in = nc.dram_tensor("w1", [NE, C, DH], F32, kind="ExternalInput").ap()
    b1T_in = nc.dram_tensor("b1T", [NE, 128, NHC], F32, kind="ExternalInput").ap()
    w2_in = nc.dram_tensor("w2", [NE, DH, C], F32, kind="ExternalInput").ap()
    b2_in = nc.dram_tensor("b2", [NE, C], F32, kind="ExternalInput").ap()
    ident_in = nc.dram_tensor("ident", [128, 128], BF16, kind="ExternalInput").ap()
    mask_in = nc.dram_tensor("mask", [128, 4 * 512], F32, kind="ExternalInput").ap()
    y_out = nc.dram_tensor("y", [B, T, C], F32, kind="ExternalOutput").ap()
    xnT = nc.dram_tensor("xnT", [C, NTOK], BF16, kind="Internal").ap()
    if debug:
        xnT_dbg = nc.dram_tensor("xnT_dbg", [C, NTOK], BF16, kind="ExternalOutput").ap()
        g_dbg = nc.dram_tensor("g_dbg", [128, NTILE * 4 * NE], F32, kind="ExternalOutput").ap()
        y0_dbg = nc.dram_tensor("y0_dbg", [B, T, C], F32, kind="ExternalOutput").ap()

    es = contextlib.ExitStack()
    with es:
        E = es.enter_context
        sb_g = E(nc.sbuf_tensor("sb_g", [128, NTILE * 4 * NE], F32))
        sb_b1 = E(nc.sbuf_tensor("sb_b1", [128, NE * NHC], F32))
        sb_id = E(nc.sbuf_tensor("sb_id", [128, 128], BF16))
        ones = E(nc.sbuf_tensor("ones", [1, 128], BF16))
        pool = E(nc.psum_tensor("pool", [128, 4096], F32))
        _sn = ["sm_pro", "sm_x", "sm_sq", "sm_veps", "sm_sd", "sm_xnt",
               "sm_tr", "sm_trc", "sm_mm", "sm_qe", "sm_gp", "sm_gev",
               "sm_xnfs", "sm_wqkv", "sm_s", "sm_max", "sm_exp", "sm_av",
               "sm_stt", "sm_xb", "sm_y0", "sm_ab", "sm_pc", "sm_w",
               "sm_xn", "sm_h", "sm_silu", "sm_o", "sm_y", "sm_acc", "sm_dve", "sm_sqr", "sm_junk"]
        (sm_pro, sm_x, sm_sq, sm_veps, sm_sd, sm_xnt, sm_tr, sm_trc, sm_mm,
         sm_qe, sm_gp, sm_gev, sm_xnfs, sm_wqkv, sm_s, sm_max, sm_exp, sm_av,
         sm_stt, sm_xb, sm_y0, sm_ab, sm_pc, sm_w, sm_xn, sm_h, sm_silu,
         sm_o, sm_y, sm_acc, sm_dve, sm_sqr, sm_junk) = (E(nc.semaphore(n)) for n in _sn)
        blk = E(nc.Block())
        ps_qkv = pool[:, 0:1024]
        ps_v = pool[:, 1024:2048]
        ps_gate = pool[:, 2048:2080]
        ps_s = pool[:, 0:2048]
        ps_av = pool[:, 2048:3072]
        ps_tr = pool[:, 3072:4096].bitcast(BF16)
        ps_h = pool[:, 0:1024]
        ps_o = pool[:, 2048:4096]

        es_ab = contextlib.ExitStack()
        with es_ab:
            EA = es_ab.enter_context
            arena = EA(nc.sbuf_tensor("arena", [128, 49152], U8))
            sb_q = EA(nc.sbuf_tensor("sb_q", [128, NCC * T], BF16))
            sb_k = EA(nc.sbuf_tensor("sb_k", [128, NCC * T], BF16))
            sb_v = EA(nc.sbuf_tensor("sb_v", [128, NQT * C], BF16))
            sb_xnf = EA(nc.sbuf_tensor("sb_xnf", [128, 2 * NCC * TT], BF16))
            sb_xntk = EA(nc.sbuf_tensor("sb_xntk", [128, 2 * C], BF16))
            sb_xa = EA(nc.sbuf_tensor("sb_xa", [128, 5 * C], F32))
            sb_xb2 = EA(nc.sbuf_tensor("sb_xb2", [128, 2 * C], F32))
            sb_y0 = EA(nc.sbuf_tensor("sb_y0", [128, 2 * C], F32))
            sb_wg = EA(nc.sbuf_tensor("sb_wg", [128, NCC * NE], BF16))
            sb_bgb = EA(nc.sbuf_tensor("sb_bgb", [128, 4 * NE], F32))
            sb_st = EA(nc.sbuf_tensor("sb_st", [128, 64], F32))
            sb_gs = EA(nc.sbuf_tensor("sb_gs", [128, 64], F32))
            sb_rs = EA(nc.sbuf_tensor("sb_rs", [128, 8], F32))
            sb_sd = EA(nc.sbuf_tensor("sb_sd", [128, 40], F32))
            wqkv_a = [arena[:, i * 16384:(i + 1) * 16384].bitcast(BF16) for i in range(3)]
            sb_p = arena[:, 0:8192].bitcast(BF16)
            sb_pT = arena[:, 8192:12288].bitcast(BF16)
            sb_msk = arena[:, 12288:20480].bitcast(F32)
            sqs = sb_xb2.bitcast(F32)

            @blk.vector
            def _(e):
                e.memset(ones[:], 1.0).then_inc(sm_pro, 1)

            @blk.sync
            def _(e):
                e.dma_start(out=sb_id[:], in_=ident_in).then_inc(sm_pro, 16)
                e.dma_start(out=sb_bgb[:], in_=bgb_in).then_inc(sm_pro, 16)
                e.dma_start(out=sb_b1[:].rearrange("p (e k) -> p e k", e=NE),
                            in_=b1T_in.rearrange("e p k -> p e k")).then_inc(sm_pro, 16)

            @blk.gpsimd
            def _(e):
                e.dma_start(out=sb_wg[:].rearrange("p (g n) -> p g n", g=NCC),
                            in_=wg_in.rearrange("(g p) n -> p g n", p=128)).then_inc(sm_wqkv, 16)
                for wi, w_in in enumerate([wq_in, wk_in, wv_in]):
                    e.dma_start(out=wqkv_a[wi].rearrange("p (g h) -> p g h", g=NCC),
                                in_=w_in.rearrange("(g p) h -> p g h", p=128)).then_inc(sm_wqkv, 16)
                for b in range(1, B):
                    e.wait_ge(sm_av, NQT * b)
                    e.wait_ge(sm_stt, NQT * b)
                    e.wait_ge(sm_exp, NQT * b)
                    e.wait_ge(sm_y0, Y0M * 16 * NQT * b)
                    for wi, w_in in enumerate([wq_in, wk_in, wv_in]):
                        e.dma_start(out=wqkv_a[wi].rearrange("p (g h) -> p g h", g=NCC),
                                    in_=w_in.rearrange("(g p) h -> p g h", p=128)).then_inc(sm_wqkv, 16)

            @blk.sync
            def _(e):
                for b in range(B):
                    def emit_loads(tt):
                        ti = b * NTILE_SEQ + tt
                        for s2 in range(4):
                            sg = ti * 4 + s2
                            sl = sg % 5
                            if sg >= 5:
                                e.wait_ge(sm_xnt, sg - 5 + 1)
                            e.dma_start(out=sb_xa[:, sl * C:(sl + 1) * C],
                                        in_=x_in[b, tt * TT + s2 * 128: tt * TT + (s2 + 1) * 128, :]
                                        ).then_inc(sm_junk, 16)
                            e.dma_start(out=sb_xa[:, (sl + 1) * C - 16:(sl + 1) * C],
                                        in_=x_in[b, tt * TT + s2 * 128: tt * TT + (s2 + 1) * 128, C - 16:C]
                                        ).then_inc(sm_x, 16)

                    def emit_store(tt):
                        ti = b * NTILE_SEQ + tt
                        e.wait_ge(sm_trc, trA(b, tt, 3) + 1)
                        tok0 = b * T + tt * TT
                        e.dma_start(
                            out=xnT[:, tok0:tok0 + TT].rearrange("(g p) t -> p g t", p=128),
                            in_=sb_xnf[:, (ti % 2) * NCC * TT:(ti % 2 + 1) * NCC * TT]
                                .rearrange("p (g t) -> p g t", g=NCC)).then_inc(sm_xnfs, 16)

                    emit_loads(0)
                    for tt in range(1, NTILE_SEQ):
                        emit_loads(tt)
                        emit_store(tt - 1)
                    emit_store(NTILE_SEQ - 1)
                    e.wait_ge(sm_gp, 4 * (b + 1))
                    e.dma_start(out=sb_msk, in_=mask_in).then_inc(sm_xb, 16)
                    for qt in range(NQT):
                        qg = NQT * b + qt
                        if qg >= 2:
                            e.wait_ge(sm_stt, qg - 1)
                        e.dma_start(out=sb_xb2[:, (qg % 2) * C:(qg % 2 + 1) * C],
                                    in_=x_in[b, qt * 128:(qt + 1) * 128, :]).then_inc(sm_xb, 16)
                        e.wait_ge(sm_stt, qg + 1)
                        e.dma_start(out=y_out[b, qt * 128:(qt + 1) * 128, :],
                                    in_=sb_y0[:, (qg % 2) * C:(qg % 2 + 1) * C]).then_inc(sm_y0, 16)
                        if debug:
                            e.dma_start(out=y0_dbg[b, qt * 128:(qt + 1) * 128, :],
                                        in_=sb_y0[:, (qg % 2) * C:(qg % 2 + 1) * C]).then_inc(sm_y0, 16)

            @blk.scalar
            def _(e):
                e.wait_ge(sm_pro, 33)

                def emit_sqrt(ti, b, tt):
                    base = 32 * (ti % 2)
                    sdb = 20 * (ti % 2)
                    e.wait_ge(sm_veps, ti + 1)
                    if b > 0 and tt == 0:
                        e.wait_ge(sm_stt, NQT * b)
                    e.activation(sb_sd[:, sdb:sdb + 20], sb_st[:, base:base + 20], AF.Sqrt)
                    e.activation(sb_sd[:, sdb + 16:sdb + 20], sb_st[:, base:base + 4],
                                 AF.Square).then_inc(sm_sd, 1)

                def emit_evacs(ti, tt, nxt=None):
                    for j in range(20):
                        if j == 10 and nxt is not None:
                            emit_sqrt(*nxt)
                        e.wait_ge(sm_mm, 20 * ti + j + 1)
                        if j < 8:
                            e.activation(sb_q[:, j * T + tt * TT: j * T + (tt + 1) * TT],
                                         ps_qkv[:, (j % 2) * 512:(j % 2 + 1) * 512],
                                         AF.Copy).then_inc(sm_qe, 1)
                        elif j < 16:
                            jj = j - 8
                            e.activation(sb_k[:, jj * T + tt * TT: jj * T + (tt + 1) * TT],
                                         ps_qkv[:, (j % 2) * 512:(j % 2 + 1) * 512],
                                         AF.Copy).then_inc(sm_qe, 1)
                        else:
                            sv = j - 16
                            e.activation(sb_v[:, (tt * 4 + sv) * C:(tt * 4 + sv + 1) * C],
                                         ps_v[:], AF.Copy).then_inc(sm_qe, 1)

                def emit_gateexp(ti):
                    e.wait_ge(sm_gev, 6 * ti + 2)
                    e.activation(sb_gs[:, 32:64], sb_gs[:, 0:32], AF.Exp)
                    e.activation(sb_sd[:, 16:20], sb_st[:, 0:4], AF.Square).then_inc(sm_qe, 4)

                for b in range(B):
                    if b > 0:
                        e.wait_ge(sm_av, NQT * b)
                    for tt in range(NTILE_SEQ):
                        ti = b * NTILE_SEQ + tt
                        if tt == 0:
                            emit_sqrt(ti, b, tt)
                        nxt = (ti + 1, b, tt + 1) if tt < NTILE_SEQ - 1 else None
                        emit_evacs(ti, tt, nxt)
                        emit_gateexp(ti)
                    for qt in range(NQT):
                        qg = NQT * b + qt
                        n = nkb(qt)
                        e.wait_ge(sm_max, qg + 1)
                        if qg >= 2:
                            e.wait_ge(sm_tr, trB((qg - 2) // NQT, (qg - 2) % NQT, nkb((qg - 2) % NQT) - 1) + 1)
                        e.activation(sb_p[:, (qg % 2) * 2048:(qg % 2) * 2048 + n * 512],
                                     ps_s[:, 0:n * 512], AF.Exp,
                                     scale=SCALE, bias=sb_rs[:, 0:1])
                        e.activation(sb_sd[:, 16:20], sb_st[:, 0:4], AF.Square).then_inc(sm_exp, 1)

            @blk.vector
            def _(e):
                dv = [0]

                def mark(op):
                    op.then_inc(sm_dve, 1)
                    dv[0] += 1

                def sync():
                    e.wait_ge(sm_dve, dv[0])

                def emit_stats(ti, b, tt):
                    base = 32 * (ti % 2)
                    for s in range(4):
                        sg = ti * 4 + s
                        e.wait_ge(sm_x, 16 * (sg + 1))
                        if b > 0 and tt == 0 and s == 0:
                            e.wait_ge(sm_stt, NQT * b)
                        sl = sg % 5
                        e.reduce_sum(sb_st[:, base + s:base + s + 1],
                                     sb_xa[:, sl * C:(sl + 1) * C], axis=AX.X)
                        mark(e.tensor_mul(sqs[:, (s % 2) * C:(s % 2 + 1) * C],
                                          sb_xa[:, sl * C:(sl + 1) * C],
                                          sb_xa[:, sl * C:(sl + 1) * C]))
                        sync()
                        r = e.reduce_sum(sb_st[:, base + 8 + s:base + 9 + s],
                                         sqs[:, (s % 2) * C:(s % 2 + 1) * C], axis=AX.X)
                        if s == 3:
                            mark(r)
                    sync()
                    mark(e.tensor_scalar_mul(sb_st[:, base + 4:base + 8],
                                             sb_st[:, base:base + 4], -1.0 / C))
                    sync()
                    mark(e.tensor_scalar(sb_st[:, base + 12:base + 16],
                                         sb_st[:, base + 8:base + 12], 1.0 / C, EPS,
                                         ALU.mult, ALU.add))
                    mark(e.tensor_mul(sb_st[:, base + 20:base + 24],
                                      sb_st[:, base + 4:base + 8], sb_st[:, base + 4:base + 8]))
                    sync()
                    e.tensor_sub(sb_st[:, base + 12:base + 16], sb_st[:, base + 12:base + 16],
                                 sb_st[:, base + 20:base + 24]).then_inc(sm_veps, 1)

                def emit_trc(pv, b):
                    for s in range(4):
                        g = trA(b, pv % NTILE_SEQ, s)
                        e.wait_ge(sm_tr, g + 1)
                        if s == 0 and pv >= 2:
                            e.wait_ge(sm_gp, pv - 1)
                            e.wait_ge(sm_xnfs, 16 * (pv - 1))
                        e.tensor_copy(
                            sb_xnf[:, (pv % 2) * NCC * TT:(pv % 2 + 1) * NCC * TT]
                            .rearrange("p (g t) -> p g t", g=NCC)[:, :, s * 128:(s + 1) * 128],
                            ps_tr[:, (g % 2) * 1024:(g % 2) * 1024 + 1024]
                            .rearrange("p (g t) -> p g t", g=NCC)).then_inc(sm_trc, 1)

                def emit_xn(ti, b):
                    base = 32 * (ti % 2)
                    e.wait_ge(sm_sd, ti + 1)
                    mark(e.reciprocal(sb_st[:, base + 20:base + 24],
                                      sb_sd[:, 20 * (ti % 2) + 12:20 * (ti % 2) + 16]))
                    sync()
                    for s in range(4):
                        sg = ti * 4 + s
                        if sg >= 2:
                            e.wait_ge(sm_tr, trA_sub(sg - 2) + 1)
                        e.tensor_scalar(sb_xntk[:, (s % 2) * C:(s % 2 + 1) * C],
                                        sb_xa[:, (sg % 5) * C:(sg % 5 + 1) * C],
                                        sb_st[:, base + 4 + s:base + 5 + s],
                                        sb_st[:, base + 20 + s:base + 21 + s],
                                        ALU.add, ALU.mult).then_inc(sm_xnt, 1)

                def emit_gate(pv):
                    base = 32 * (pv % 2)
                    e.wait_ge(sm_gp, pv + 1)
                    if pv > 0:
                        e.wait_ge(sm_gev, 6 * pv)
                    e.tensor_copy(sb_gs[:, 0:32], ps_gate[:]).then_inc(sm_gev, 1)
                    e.wait_ge(sm_gev, 6 * pv + 1)
                    e.tensor_add(sb_gs[:, 0:32], sb_gs[:, 0:32], sb_bgb[:]).then_inc(sm_gev, 1)
                    for s in range(4):
                        e.wait_ge(sm_qe, 24 * pv + 20 + s + 1)
                        mark(e.reduce_sum(sb_st[:, base + 28 + s:base + 29 + s],
                                          sb_gs[:, 32 + s * 8:32 + (s + 1) * 8], axis=AX.X))
                        sync()
                        e.reciprocal(sb_st[:, base + 24 + s:base + 25 + s],
                                     sb_st[:, base + 28 + s:base + 29 + s])
                        mark(e.reduce_max(sb_st[:, base + 28 + s:base + 29 + s],
                                          sb_gs[:, 32 + s * 8:32 + (s + 1) * 8], axis=AX.X))
                        sync()
                        mark(e.tensor_scalar(sb_gs[:, s * 8:(s + 1) * 8],
                                             sb_gs[:, 32 + s * 8:32 + (s + 1) * 8],
                                             sb_st[:, base + 28 + s:base + 29 + s], None, ALU.is_ge))
                        sync()
                        e.scalar_tensor_tensor(sb_g[:, (pv * 4 + s) * 8:(pv * 4 + s + 1) * 8],
                                               sb_gs[:, 32 + s * 8:32 + (s + 1) * 8],
                                               sb_st[:, base + 24 + s:base + 25 + s],
                                               sb_gs[:, s * 8:(s + 1) * 8],
                                               ALU.mult, ALU.mult).then_inc(sm_gev, 1)

                for b in range(B):
                    for tt in range(NTILE_SEQ):
                        ti = b * NTILE_SEQ + tt
                        emit_stats(ti, b, tt)
                        if tt > 0:
                            emit_trc(ti - 1, b)
                        emit_xn(ti, b)
                        if tt > 0:
                            emit_gate(ti - 1)
                    emit_trc(b * NTILE_SEQ + 3, b)
                    emit_gate(b * NTILE_SEQ + 3)
                    for qt in range(NQT):
                        qg = NQT * b + qt
                        n = nkb(qt)
                        if qt == 0:
                            e.wait_ge(sm_xb, 16 * (NQT * b + b + 1))
                        e.wait_ge(sm_s, sgi(b, qt, n - 1) + 1)
                        mark(e.tensor_add(ps_s[:, (n - 1) * 512:n * 512],
                                          ps_s[:, (n - 1) * 512:n * 512],
                                          sb_msk[:, (qt % 4) * 512:(qt % 4 + 1) * 512]))
                        sync()
                        mark(e.reduce_max(sb_rs[:, 0:1], ps_s[:, 0:n * 512], axis=AX.X, negate=True))
                        sync()
                        e.tensor_scalar_mul(sb_rs[:, 0:1], sb_rs[:, 0:1], SCALE).then_inc(sm_max, 1)
                        for kb in range(n):
                            g = trB(b, qt, kb)
                            e.wait_ge(sm_tr, g + 1)
                            if kb == 0 and qg >= 1:
                                e.wait_ge(sm_av, qg)
                            e.tensor_copy(sb_pT[:, kb * 512:(kb + 1) * 512],
                                          ps_tr[:, (g % 2) * 1024:(g % 2 + 1) * 1024][:, 0:512]
                                          ).then_inc(sm_trc, 1)
                        e.wait_ge(sm_exp, qg + 1)
                        mark(e.reduce_sum(sb_rs[:, 1:2],
                                          sb_p[:, (qg % 2) * 2048:(qg % 2) * 2048 + n * 512],
                                          axis=AX.X))
                        sync()
                        mark(e.reciprocal(sb_rs[:, 2:3], sb_rs[:, 1:2]))
                        sync()
                        e.wait_ge(sm_av, qg + 1)
                        e.wait_ge(sm_xb, 16 * (qg + 1) + 16 * (b + 1))
                        if qg >= 2:
                            e.wait_ge(sm_y0, Y0M * 16 * (qg - 1))
                        e.scalar_tensor_tensor(sb_y0[:, (qg % 2) * C:(qg % 2 + 1) * C],
                                               ps_av[:], sb_rs[:, 2:3],
                                               sb_xb2[:, (qg % 2) * C:(qg % 2 + 1) * C],
                                               ALU.mult, ALU.add).then_inc(sm_stt, 1)

            @blk.tensor
            def _(e):
                e.wait_ge(sm_pro, 33)
                for b in range(B):
                    for tt in range(NTILE_SEQ):
                        ti = b * NTILE_SEQ + tt
                        for s in range(4):
                            g = trA(b, tt, s)
                            e.wait_ge(sm_xnt, ti * 4 + s + 1)
                            if g >= 2:
                                e.wait_ge(sm_trc, g - 1)
                            for c in range(NCC):
                                tr = e.transpose(
                                    ps_tr[:, (g % 2) * 1024 + c * 128:(g % 2) * 1024 + (c + 1) * 128],
                                    sb_xntk[:, (s % 2) * C + c * 128:(s % 2) * C + (c + 1) * 128],
                                    sb_id[:])
                                if c == NCC - 1:
                                    tr.then_inc(sm_tr, 1)
                        e.wait_ge(sm_trc, trA(b, tt, 3) + 1)
                        if b == 0 and tt == 0:
                            e.wait_ge(sm_wqkv, 64)
                        if b > 0 and tt == 0:
                            e.wait_ge(sm_wqkv, 64 + 48 * b)
                            e.wait_ge(sm_exp, NQT * b)
                        xf = sb_xnf[:, (ti % 2) * NCC * TT:(ti % 2 + 1) * NCC * TT]
                        jj = 0
                        for wi, wbuf in enumerate([wqkv_a[0], wqkv_a[1]]):
                            for m in range(8):
                                j = jj
                                jj += 1
                                e.wait_ge(sm_qe, 24 * ti + j - 1) if (24 * ti + j) >= 2 else None
                                pb = ps_qkv[:, (j % 2) * 512:(j % 2 + 1) * 512]
                                for c in range(NCC):
                                    mm = e.matmul(pb, wbuf[:, c * C + m * 128:c * C + (m + 1) * 128],
                                                  xf[:, c * TT:(c + 1) * TT],
                                                  start=(c == 0), stop=(c == NCC - 1))
                                    if c == NCC - 1:
                                        mm.then_inc(sm_mm, 1)
                        for s in range(4):
                            j = 16 + s
                            e.wait_ge(sm_qe, 24 * ti + j - 1 + 1) if (ti > 0 or s > 0) else None
                            for c in range(NCC):
                                for hh in range(2):
                                    mm = e.matmul(ps_v[:, hh * 512:(hh + 1) * 512],
                                                  xf[:, c * TT + s * 128:c * TT + (s + 1) * 128],
                                                  wqkv_a[2][:, c * C + hh * 512:c * C + (hh + 1) * 512],
                                                  start=(c == 0), stop=(c == NCC - 1))
                                    if c == NCC - 1 and hh == 1:
                                        mm.then_inc(sm_mm, 1)
                        if ti > 0:
                            e.wait_ge(sm_gev, 6 * (ti - 1) + 1)
                        if b > 0 and tt == 0:
                            e.wait_ge(sm_stt, NQT * b)
                        for s in range(4):
                            for c in range(NCC):
                                mm = e.matmul(ps_gate[:, s * 8:(s + 1) * 8],
                                              xf[:, c * TT + s * 128:c * TT + (s + 1) * 128],
                                              sb_wg[:, c * 8:(c + 1) * 8],
                                              start=(s == 0 and c == 0), stop=(c == NCC - 1))
                                if s == 3 and c == NCC - 1:
                                    mm.then_inc(sm_gp, 1)
                    for qt in range(NQT):
                        qg = NQT * b + qt
                        n = nkb(qt)
                        for kb in range(n):
                            if qg > 0 and kb == 0:
                                e.wait_ge(sm_exp, qg)
                            if qt == 0 and kb == 0:
                                e.wait_ge(sm_qe, 96 * (b + 1))
                            for c in range(8):
                                mm = e.matmul(ps_s[:, kb * 512:(kb + 1) * 512],
                                              sb_q[:, c * T + qt * 128:c * T + (qt + 1) * 128],
                                              sb_k[:, c * T + kb * 512:c * T + (kb + 1) * 512],
                                              start=(c == 0), stop=(c == 7))
                                if c == 7:
                                    mm.then_inc(sm_s, 1)
                        e.wait_ge(sm_exp, qg + 1)
                        for kb in range(n):
                            g = trB(b, qt, kb)
                            if g >= 2:
                                e.wait_ge(sm_trc, g - 1)
                            for u in range(4):
                                tr = e.transpose(
                                    ps_tr[:, (g % 2) * 1024 + u * 128:(g % 2) * 1024 + (u + 1) * 128],
                                    sb_p[:, (qg % 2) * 2048 + kb * 512 + u * 128:
                                         (qg % 2) * 2048 + kb * 512 + (u + 1) * 128],
                                    sb_id[:])
                                if u == 3:
                                    tr.then_inc(sm_tr, 1)
                        e.wait_ge(sm_trc, trB(b, qt, n - 1) + 1)
                        if qg > 0:
                            e.wait_ge(sm_stt, qg)
                        if qt == 0:
                            e.wait_ge(sm_gev, 6 * (4 * b + 3) + 1)
                        for sc in range(n * 4):
                            for hh in range(2):
                                mm = e.matmul(ps_av[:, hh * 512:(hh + 1) * 512],
                                              sb_pT[:, sc * 128:(sc + 1) * 128],
                                              sb_v[:, sc * C + hh * 512:sc * C + (hh + 1) * 512],
                                              start=(sc == 0), stop=(sc == n * 4 - 1))
                                if sc == n * 4 - 1 and hh == 1:
                                    mm.then_inc(sm_av, 1)

        es_c = contextlib.ExitStack()
        with es_c:
            EC = es_c.enter_context
            sb_w1 = EC(nc.sbuf_tensor("sb_w1", [128, NCC * DH], BF16))
            sb_w2 = EC(nc.sbuf_tensor("sb_w2", [128, NHC * C], BF16))
            sb_xnc = EC(nc.sbuf_tensor("sb_xnc", [128, 2 * NCC * TT], BF16))
            sb_h = EC(nc.sbuf_tensor("sb_h", [128, NHC * TT], BF16))
            sb_y = EC(nc.sbuf_tensor("sb_y", [128, 2 * 2 * C], F32))
            sb_b2 = EC(nc.sbuf_tensor("sb_b2", [1, C], BF16))
            @blk.sync
            def _(e):
                e.wait_ge(sm_exp, NQT * B)
                e.wait_ge(sm_stt, NQT * B)
                e.wait_ge(sm_av, NQT * B)
                e.wait_ge(sm_y0, Y0M * 16 * NQT * B)
                for ex in range(NE):
                    for tt in range(NTILE):
                        i = ex * NTILE + tt
                        if i >= 2:
                            e.wait_ge(sm_h, NHC * (i - 1))
                        tok0 = tt * TT
                        src = xnT[:, tok0:tok0 + TT].rearrange("(g p) t -> p g t", p=128)
                        dst = sb_xnc[:, (i % 2) * NCC * TT:(i % 2 + 1) * NCC * TT]
                        e.dma_start(out=dst.rearrange("p (g t) -> p g t", g=NCC),
                                    in_=src).then_inc(sm_xn, 16)
                if debug:
                    e.wait_ge(sm_xnfs, 16 * NTILE)
                    e.dma_start(out=xnT_dbg, in_=xnT).then_inc(sm_xn, 16)
                    e.dma_start(out=g_dbg, in_=sb_g[:]).then_inc(sm_xn, 16)

            @blk.tensor
            def _(e):
                for ex in range(NE):
                    for tt in range(NTILE):
                        i = ex * NTILE + tt
                        e.wait_ge(sm_xn, 16 * (i + 1))
                        xt = sb_xnc[:, (i % 2) * NCC * TT:(i % 2 + 1) * NCC * TT]
                        for k in range(NHC):
                            gi = i * NHC + k
                            if tt == 0 and k == 0:
                                e.wait_ge(sm_w, 80 * ex + 16)
                            if tt == 0 and k == NHC // 2:
                                e.wait_ge(sm_w, 80 * ex + 32)
                            if gi >= 2:
                                e.wait_ge(sm_silu, gi - 1)
                            pb = ps_h[:, (gi % 2) * TT:(gi % 2 + 1) * TT]
                            for c in range(NCC):
                                mm = e.matmul(pb, sb_w1[:, (c * DH + k * 128):(c * DH + (k + 1) * 128)],
                                              xt[:, c * TT:(c + 1) * TT],
                                              start=(c == 0), stop=(c == NCC - 1))
                                if c == NCC - 1:
                                    mm.then_inc(sm_h, 1)
                        if tt == 0:
                            e.wait_ge(sm_w, 80 * (ex + 1))
                        e.wait_ge(sm_silu, (i + 1) * NHC)
                        for s in range(4):
                            oi = i * 4 + s
                            if oi >= 2:
                                e.wait_ge(sm_y, oi - 1)
                            po = ps_o[:, (oi % 2) * C:(oi % 2 + 1) * C]
                            for k in range(NHC):
                                hT = sb_h[:, (k * TT + s * 128):(k * TT + (s + 1) * 128)]
                                e.matmul(po[:, 0:512], hT, sb_w2[:, k * C:k * C + 512],
                                         start=(k == 0), stop=False)
                                e.matmul(po[:, 512:1024], hT, sb_w2[:, k * C + 512:(k + 1) * C],
                                         start=(k == 0), stop=False)
                            e.matmul(po[:, 0:512], ones[0:1, 0:128], sb_b2[0:1, 0:512],
                                     start=False, stop=True)
                            e.matmul(po[:, 512:1024], ones[0:1, 0:128], sb_b2[0:1, 512:1024],
                                     start=False, stop=True).then_inc(sm_o, 1)

            @blk.scalar
            def _(e):
                for ex in range(NE):
                    for tt in range(NTILE):
                        i = ex * NTILE + tt
                        for k in range(NHC):
                            gi = i * NHC + k
                            e.wait_ge(sm_h, gi + 1)
                            e.activation(sb_h[:, k * TT:(k + 1) * TT],
                                         ps_h[:, (gi % 2) * TT:(gi % 2 + 1) * TT],
                                         AF.Silu, bias=sb_b1[:, ex * NHC + k:ex * NHC + k + 1]
                                         ).then_inc(sm_silu, 1)

            @blk.vector
            def _(e):
                for ex in range(NE):
                    for tt in range(NTILE):
                        i = ex * NTILE + tt
                        for s in range(4):
                            oi = i * 4 + s
                            e.wait_ge(sm_o, oi + 1)
                            if oi >= 4:
                                e.wait_ge(sm_acc, 16 * (oi // 2 - 1))
                            gcol = sb_g[:, (tt * 4 + s) * NE + ex:(tt * 4 + s) * NE + ex + 1]
                            j = oi // 2
                            e.tensor_scalar(sb_y[:, (j % 2) * 2 * C + (s % 2) * C:
                                                 (j % 2) * 2 * C + (s % 2 + 1) * C],
                                            ps_o[:, (oi % 2) * C:(oi % 2 + 1) * C], gcol, None,
                                            ALU.mult).then_inc(sm_y, 1)

            @blk.gpsimd
            def _(e):
                def load_w(ex):
                    if ex > 0:
                        e.wait_ge(sm_h, NHC * NTILE * ex)
                    for half in range(2):
                        e.dma_start(
                            out=sb_w1[:].rearrange("p (g h) -> p g h", g=NCC)[:, :, half * 2048:(half + 1) * 2048],
                            in_=w1_in[ex].rearrange("(g p) h -> p g h", p=128)[:, :, half * 2048:(half + 1) * 2048]
                        ).then_inc(sm_w, 16)

                def load_w2(ex):
                    if ex > 0:
                        e.wait_ge(sm_o, NTILE * 4 * ex)
                    for half in range(2):
                        e.dma_start(
                            out=sb_w2[:].rearrange("p (g d) -> p g d", g=NHC)[:, half * 16:(half + 1) * 16, :],
                            in_=w2_in[ex].rearrange("(g p) d -> p g d", p=128)[:, half * 16:(half + 1) * 16, :]
                        ).then_inc(sm_w, 16)
                    e.dma_start(out=sb_b2[:], in_=b2_in[ex].unsqueeze(0)).then_inc(sm_w, 16)

                e.wait_ge(sm_exp, NQT * B)
                e.wait_ge(sm_stt, NQT * B)
                e.wait_ge(sm_av, NQT * B)
                e.wait_ge(sm_y0, Y0M * 16 * NQT * B)
                load_w(0)
                load_w2(0)
                for ex in range(NE):
                    for tt in range(NTILE):
                        for half in range(2):
                            a = (ex * NTILE + tt) * 2 + half
                            e.wait_ge(sm_y, a * 2 + 2)
                            b = tt // NTILE_SEQ
                            r0 = (tt % NTILE_SEQ) * TT + half * 256
                            dst = y_out[b, r0:r0 + 256, :]
                            e.dma_start(out=dst.rearrange("(s p) d -> p s d", p=128),
                                        in_=sb_y[:, (a % 2) * 2 * C:(a % 2 + 1) * 2 * C]
                                        .rearrange("p (s d) -> p s d", s=2),
                                        accum_op=ALU.add).then_inc(sm_acc, 16)
                        if tt == 14 and ex < NE - 1:
                            load_w(ex + 1)
                    if ex < NE - 1:
                        load_w2(ex + 1)
                e.wait_ge(sm_acc, 16 * NE * NTILE * 2)

    return nc


def _host_inputs(shard, ins):
    bf = ml_dtypes.bfloat16
    b1 = np.asarray(ins["L0_ffn_b1"], dtype=np.float32)
    m = {
        "x": np.ascontiguousarray(shard),
        "wq": np.ascontiguousarray(np.asarray(ins["L0_attn_Wq"], dtype=np.float32)),
        "wk": np.ascontiguousarray(np.asarray(ins["L0_attn_Wk"], dtype=np.float32)),
        "wv": np.ascontiguousarray(np.asarray(ins["L0_attn_Wv"], dtype=np.float32)),
        "wg": np.ascontiguousarray(np.asarray(ins["L0_ffn_Wg"], dtype=np.float32)),
        "bgb": np.ascontiguousarray(np.tile(np.asarray(ins["L0_ffn_bg"], dtype=np.float32), (128, 4))),
        "w1": np.ascontiguousarray(np.asarray(ins["L0_ffn_W1"], dtype=np.float32)),
        "b1T": np.ascontiguousarray(b1.reshape(NE, NHC, 128).transpose(0, 2, 1)),
        "w2": np.ascontiguousarray(np.asarray(ins["L0_ffn_W2"], dtype=np.float32)),
        "b2": np.ascontiguousarray(np.asarray(ins["L0_ffn_b2"], dtype=np.float32)),
        "ident": np.eye(128, dtype=bf),
    }
    p = np.arange(128)[:, None]
    col = np.arange(512)[None, :]
    mask = np.zeros((128, 4 * 512), dtype=np.float32)
    for r in range(4):
        mask[:, r * 512:(r + 1) * 512] = np.where(col <= r * 128 + p, 0.0, NEG)
    m["mask"] = mask
    return m


def kernel(**inputs):
    x = np.asarray(inputs["L0_attn_x"], dtype=np.float32)
    shards = np.split(np.ascontiguousarray(x), 8, axis=0)
    nc = build_nc()
    in_maps = [_host_inputs(s, inputs) for s in shards]
    res = run_bass_kernel_spmd(nc, in_maps, core_ids=list(range(8)))
    return np.concatenate([np.asarray(r["y"]) for r in res.results], axis=0)


# version 13 (final)
# speedup vs baseline: 1.2086x; 1.1587x over previous
import contextlib
import numpy as np
import ml_dtypes
import concourse.bass as bass
import concourse.mybir as mybir
from concourse.bass_utils import run_bass_kernel_spmd

F32 = mybir.dt.float32
BF16 = mybir.dt.bfloat16
U8 = mybir.dt.uint8
AF = mybir.ActivationFunctionType
ALU = mybir.AluOpType
AX = mybir.AxisListType

B = 4
T = 2048
C = 1024
DH = 4096
NE = 8
NCC = C // 128
NHC = DH // 128
TT = 512
NTILE_SEQ = T // TT
NTILE = B * NTILE_SEQ
NTOK = B * T
NQT = T // 128
NEG = -1.0e30
EPS = 1e-5
SCALE = 1.0 / 32.0


def nkb(qt):
    return qt // 4 + 1


_PB_PRE = [0]
for _q in range(NQT):
    _PB_PRE.append(_PB_PRE[-1] + nkb(_q))
TR_PER_SEQ = 16 + _PB_PRE[NQT]


def trA(b, tt, s):
    return TR_PER_SEQ * b + tt * 4 + s


def trA_sub(sg):
    return TR_PER_SEQ * (sg // 16) + (sg % 16)


def trB(b, qt, kb):
    return TR_PER_SEQ * b + 16 + _PB_PRE[qt] + kb


def sgi(b, qt, kb):
    return _PB_PRE[NQT] * b + _PB_PRE[qt] + kb


def build_nc(debug=False):
    Y0M = 2 if debug else 1
    nc = bass.Bass("TRN2", target_bir_lowering=False)
    x_in = nc.dram_tensor("x", [B, T, C], F32, kind="ExternalInput").ap()
    wq_in = nc.dram_tensor("wq", [C, C], F32, kind="ExternalInput").ap()
    wk_in = nc.dram_tensor("wk", [C, C], F32, kind="ExternalInput").ap()
    wv_in = nc.dram_tensor("wv", [C, C], F32, kind="ExternalInput").ap()
    wg_in = nc.dram_tensor("wg", [C, NE], F32, kind="ExternalInput").ap()
    bgb_in = nc.dram_tensor("bgb", [128, 4 * NE], F32, kind="ExternalInput").ap()
    w1_in = nc.dram_tensor("w1", [NE, C, DH], F32, kind="ExternalInput").ap()
    b1T_in = nc.dram_tensor("b1T", [NE, 128, NHC], F32, kind="ExternalInput").ap()
    w2_in = nc.dram_tensor("w2", [NE, DH, C], F32, kind="ExternalInput").ap()
    b2_in = nc.dram_tensor("b2", [NE, C], F32, kind="ExternalInput").ap()
    ident_in = nc.dram_tensor("ident", [128, 128], BF16, kind="ExternalInput").ap()
    mask_in = nc.dram_tensor("mask", [128, 4 * 512], F32, kind="ExternalInput").ap()
    y_out = nc.dram_tensor("y", [B, T, C], F32, kind="ExternalOutput").ap()
    xnT = nc.dram_tensor("xnT", [C, NTOK], BF16, kind="Internal").ap()
    if debug:
        xnT_dbg = nc.dram_tensor("xnT_dbg", [C, NTOK], BF16, kind="ExternalOutput").ap()
        g_dbg = nc.dram_tensor("g_dbg", [128, NTILE * 4 * NE], F32, kind="ExternalOutput").ap()
        y0_dbg = nc.dram_tensor("y0_dbg", [B, T, C], F32, kind="ExternalOutput").ap()

    es = contextlib.ExitStack()
    with es:
        E = es.enter_context
        sb_g = E(nc.sbuf_tensor("sb_g", [128, NTILE * 4 * NE], F32))
        sb_b1 = E(nc.sbuf_tensor("sb_b1", [128, NE * NHC], F32))
        sb_id = E(nc.sbuf_tensor("sb_id", [128, 128], BF16))
        ones = E(nc.sbuf_tensor("ones", [1, 128], BF16))
        pool = E(nc.psum_tensor("pool", [128, 4096], F32))
        _sn = ["sm_pro", "sm_x", "sm_sq", "sm_veps", "sm_sd", "sm_xnt",
               "sm_tr", "sm_trc", "sm_mm", "sm_qe", "sm_gp", "sm_gev",
               "sm_xnfs", "sm_wqkv", "sm_s", "sm_max", "sm_exp", "sm_av",
               "sm_stt", "sm_xb", "sm_y0", "sm_ab", "sm_pc", "sm_w",
               "sm_xn", "sm_h", "sm_silu", "sm_o", "sm_y", "sm_acc", "sm_dve", "sm_sqr", "sm_junk"]
        (sm_pro, sm_x, sm_sq, sm_veps, sm_sd, sm_xnt, sm_tr, sm_trc, sm_mm,
         sm_qe, sm_gp, sm_gev, sm_xnfs, sm_wqkv, sm_s, sm_max, sm_exp, sm_av,
         sm_stt, sm_xb, sm_y0, sm_ab, sm_pc, sm_w, sm_xn, sm_h, sm_silu,
         sm_o, sm_y, sm_acc, sm_dve, sm_sqr, sm_junk) = (E(nc.semaphore(n)) for n in _sn)
        blk = E(nc.Block())
        ps_qkv = pool[:, 0:1024]
        ps_v = pool[:, 1024:2048]
        ps_gate = pool[:, 2048:2080]
        ps_s = pool[:, 0:2048]
        ps_av = pool[:, 2048:3072]
        ps_tr = pool[:, 3072:4096].bitcast(BF16)
        ps_h = pool[:, 0:2048]
        ps_o = pool[:, 2048:4096]

        es_ab = contextlib.ExitStack()
        with es_ab:
            EA = es_ab.enter_context
            arena = EA(nc.sbuf_tensor("arena", [128, 49152], U8))
            sb_q = EA(nc.sbuf_tensor("sb_q", [128, NCC * T], BF16))
            sb_k = EA(nc.sbuf_tensor("sb_k", [128, NCC * T], BF16))
            sb_v = EA(nc.sbuf_tensor("sb_v", [128, NQT * C], BF16))
            sb_xnf = EA(nc.sbuf_tensor("sb_xnf", [128, 2 * NCC * TT], BF16))
            sb_xntk = EA(nc.sbuf_tensor("sb_xntk", [128, 2 * C], BF16))
            sb_xa = EA(nc.sbuf_tensor("sb_xa", [128, 5 * C], F32))
            sb_xb2 = EA(nc.sbuf_tensor("sb_xb2", [128, 2 * C], F32))
            sb_y0 = EA(nc.sbuf_tensor("sb_y0", [128, 2 * C], F32))
            sb_wg = EA(nc.sbuf_tensor("sb_wg", [128, NCC * NE], BF16))
            sb_bgb = EA(nc.sbuf_tensor("sb_bgb", [128, 4 * NE], F32))
            sb_st = EA(nc.sbuf_tensor("sb_st", [128, 64], F32))
            sb_gs = EA(nc.sbuf_tensor("sb_gs", [128, 64], F32))
            sb_rs = EA(nc.sbuf_tensor("sb_rs", [128, 8], F32))
            sb_sd = EA(nc.sbuf_tensor("sb_sd", [128, 20], F32))
            wqkv_a = [arena[:, i * 16384:(i + 1) * 16384].bitcast(BF16) for i in range(3)]
            sb_p = arena[:, 0:8192].bitcast(BF16)
            sb_pT = arena[:, 8192:12288].bitcast(BF16)
            sb_msk = arena[:, 12288:20480].bitcast(F32)
            sqs = sb_xb2.bitcast(F32)

            @blk.vector
            def _(e):
                e.memset(ones[:], 1.0).then_inc(sm_pro, 1)

            @blk.sync
            def _(e):
                e.dma_start(out=sb_id[:], in_=ident_in).then_inc(sm_pro, 16)
                e.dma_start(out=sb_bgb[:], in_=bgb_in).then_inc(sm_pro, 16)
                e.dma_start(out=sb_b1[:].rearrange("p (e k) -> p e k", e=NE),
                            in_=b1T_in.rearrange("e p k -> p e k")).then_inc(sm_pro, 16)

            @blk.gpsimd
            def _(e):
                e.dma_start(out=sb_wg[:].rearrange("p (g n) -> p g n", g=NCC),
                            in_=wg_in.rearrange("(g p) n -> p g n", p=128)).then_inc(sm_wqkv, 16)
                for wi, w_in in enumerate([wq_in, wk_in, wv_in]):
                    e.dma_start(out=wqkv_a[wi].rearrange("p (g h) -> p g h", g=NCC),
                                in_=w_in.rearrange("(g p) h -> p g h", p=128)).then_inc(sm_wqkv, 16)
                for b in range(1, B):
                    e.wait_ge(sm_av, NQT * b)
                    e.wait_ge(sm_stt, NQT * b)
                    e.wait_ge(sm_exp, NQT * b)
                    e.wait_ge(sm_y0, Y0M * 16 * NQT * b)
                    for wi, w_in in enumerate([wq_in, wk_in, wv_in]):
                        e.dma_start(out=wqkv_a[wi].rearrange("p (g h) -> p g h", g=NCC),
                                    in_=w_in.rearrange("(g p) h -> p g h", p=128)).then_inc(sm_wqkv, 16)

            @blk.sync
            def _(e):
                for b in range(B):
                    def emit_loads(tt):
                        ti = b * NTILE_SEQ + tt
                        for s2 in range(4):
                            sg = ti * 4 + s2
                            sl = sg % 5
                            if sg >= 5:
                                e.wait_ge(sm_xnt, sg - 5 + 1)
                            e.dma_start(out=sb_xa[:, sl * C:(sl + 1) * C],
                                        in_=x_in[b, tt * TT + s2 * 128: tt * TT + (s2 + 1) * 128, :]
                                        ).then_inc(sm_junk, 16)
                            e.dma_start(out=sb_xa[:, (sl + 1) * C - 16:(sl + 1) * C],
                                        in_=x_in[b, tt * TT + s2 * 128: tt * TT + (s2 + 1) * 128, C - 16:C]
                                        ).then_inc(sm_x, 16)

                    def emit_store(tt):
                        ti = b * NTILE_SEQ + tt
                        e.wait_ge(sm_trc, trA(b, tt, 3) + 1)
                        tok0 = b * T + tt * TT
                        e.dma_start(
                            out=xnT[:, tok0:tok0 + TT].rearrange("(g p) t -> p g t", p=128),
                            in_=sb_xnf[:, (ti % 2) * NCC * TT:(ti % 2 + 1) * NCC * TT]
                                .rearrange("p (g t) -> p g t", g=NCC)).then_inc(sm_xnfs, 16)

                    emit_loads(0)
                    for tt in range(1, NTILE_SEQ):
                        emit_loads(tt)
                        emit_store(tt - 1)
                    emit_store(NTILE_SEQ - 1)
                    e.wait_ge(sm_gp, 4 * (b + 1))
                    e.dma_start(out=sb_msk, in_=mask_in).then_inc(sm_xb, 16)
                    for qt in range(NQT):
                        qg = NQT * b + qt
                        if qg >= 2:
                            e.wait_ge(sm_stt, qg - 1)
                        e.dma_start(out=sb_xb2[:, (qg % 2) * C:(qg % 2 + 1) * C],
                                    in_=x_in[b, qt * 128:(qt + 1) * 128, :]).then_inc(sm_xb, 16)
                        e.wait_ge(sm_stt, qg + 1)
                        e.dma_start(out=y_out[b, qt * 128:(qt + 1) * 128, :],
                                    in_=sb_y0[:, (qg % 2) * C:(qg % 2 + 1) * C]).then_inc(sm_y0, 16)
                        if debug:
                            e.dma_start(out=y0_dbg[b, qt * 128:(qt + 1) * 128, :],
                                        in_=sb_y0[:, (qg % 2) * C:(qg % 2 + 1) * C]).then_inc(sm_y0, 16)

            @blk.scalar
            def _(e):
                e.wait_ge(sm_pro, 33)

                def emit_sqrt(ti, b, tt):
                    base = 32 * (ti % 2)
                    e.wait_ge(sm_veps, ti + 1)
                    if b > 0 and tt == 0:
                        e.wait_ge(sm_stt, NQT * b)
                    e.activation(sb_sd[:], sb_st[:, base:base + 20], AF.Sqrt)
                    e.activation(sb_sd[:, 16:20], sb_st[:, base:base + 4], AF.Square).then_inc(sm_sd, 1)

                def emit_evacs(ti, tt):
                    for j in range(20):
                        e.wait_ge(sm_mm, 20 * ti + j + 1)
                        if j < 8:
                            e.activation(sb_q[:, j * T + tt * TT: j * T + (tt + 1) * TT],
                                         ps_qkv[:, (j % 2) * 512:(j % 2 + 1) * 512],
                                         AF.Copy).then_inc(sm_qe, 1)
                        elif j < 16:
                            jj = j - 8
                            e.activation(sb_k[:, jj * T + tt * TT: jj * T + (tt + 1) * TT],
                                         ps_qkv[:, (j % 2) * 512:(j % 2 + 1) * 512],
                                         AF.Copy).then_inc(sm_qe, 1)
                        else:
                            sv = j - 16
                            e.activation(sb_v[:, (tt * 4 + sv) * C:(tt * 4 + sv + 1) * C],
                                         ps_v[:], AF.Copy).then_inc(sm_qe, 1)

                def emit_gateexp(ti):
                    e.wait_ge(sm_gev, 6 * ti + 2)
                    e.activation(sb_gs[:, 32:64], sb_gs[:, 0:32], AF.Exp)
                    e.activation(sb_sd[:, 16:20], sb_st[:, 0:4], AF.Square).then_inc(sm_qe, 4)

                for b in range(B):
                    if b > 0:
                        e.wait_ge(sm_av, NQT * b)
                    for tt in range(NTILE_SEQ):
                        ti = b * NTILE_SEQ + tt
                        if tt == 0:
                            emit_sqrt(ti, b, tt)
                        emit_evacs(ti, tt)
                        if tt < NTILE_SEQ - 1:
                            emit_sqrt(ti + 1, b, tt + 1)
                        emit_gateexp(ti)
                    for qt in range(NQT):
                        qg = NQT * b + qt
                        n = nkb(qt)
                        e.wait_ge(sm_max, qg + 1)
                        if qg >= 2:
                            e.wait_ge(sm_tr, trB((qg - 2) // NQT, (qg - 2) % NQT, nkb((qg - 2) % NQT) - 1) + 1)
                        e.activation(sb_p[:, (qg % 2) * 2048:(qg % 2) * 2048 + n * 512],
                                     ps_s[:, 0:n * 512], AF.Exp,
                                     scale=SCALE, bias=sb_rs[:, 0:1])
                        e.activation(sb_sd[:, 16:20], sb_st[:, 0:4], AF.Square).then_inc(sm_exp, 1)

            @blk.vector
            def _(e):
                dv = [0]

                def mark(op):
                    op.then_inc(sm_dve, 1)
                    dv[0] += 1

                def sync():
                    e.wait_ge(sm_dve, dv[0])

                def emit_stats(ti, b, tt):
                    base = 32 * (ti % 2)
                    for s in range(4):
                        sg = ti * 4 + s
                        e.wait_ge(sm_x, 16 * (sg + 1))
                        if b > 0 and tt == 0 and s == 0:
                            e.wait_ge(sm_stt, NQT * b)
                        sl = sg % 5
                        e.reduce_sum(sb_st[:, base + s:base + s + 1],
                                     sb_xa[:, sl * C:(sl + 1) * C], axis=AX.X)
                        mark(e.tensor_mul(sqs[:, (s % 2) * C:(s % 2 + 1) * C],
                                          sb_xa[:, sl * C:(sl + 1) * C],
                                          sb_xa[:, sl * C:(sl + 1) * C]))
                        sync()
                        r = e.reduce_sum(sb_st[:, base + 8 + s:base + 9 + s],
                                         sqs[:, (s % 2) * C:(s % 2 + 1) * C], axis=AX.X)
                        if s == 3:
                            mark(r)
                    sync()
                    mark(e.tensor_scalar_mul(sb_st[:, base + 4:base + 8],
                                             sb_st[:, base:base + 4], -1.0 / C))
                    sync()
                    mark(e.tensor_scalar(sb_st[:, base + 12:base + 16],
                                         sb_st[:, base + 8:base + 12], 1.0 / C, EPS,
                                         ALU.mult, ALU.add))
                    mark(e.tensor_mul(sb_st[:, base + 20:base + 24],
                                      sb_st[:, base + 4:base + 8], sb_st[:, base + 4:base + 8]))
                    sync()
                    e.tensor_sub(sb_st[:, base + 12:base + 16], sb_st[:, base + 12:base + 16],
                                 sb_st[:, base + 20:base + 24]).then_inc(sm_veps, 1)

                def emit_trc(pv, b):
                    for s in range(4):
                        g = trA(b, pv % NTILE_SEQ, s)
                        e.wait_ge(sm_tr, g + 1)
                        if s == 0 and pv >= 2:
                            e.wait_ge(sm_gp, pv - 1)
                            e.wait_ge(sm_xnfs, 16 * (pv - 1))
                        e.tensor_copy(
                            sb_xnf[:, (pv % 2) * NCC * TT:(pv % 2 + 1) * NCC * TT]
                            .rearrange("p (g t) -> p g t", g=NCC)[:, :, s * 128:(s + 1) * 128],
                            ps_tr[:, (g % 2) * 1024:(g % 2) * 1024 + 1024]
                            .rearrange("p (g t) -> p g t", g=NCC)).then_inc(sm_trc, 1)

                def emit_xn(ti, b):
                    base = 32 * (ti % 2)
                    e.wait_ge(sm_sd, ti + 1)
                    mark(e.reciprocal(sb_st[:, base + 20:base + 24], sb_sd[:, 12:16]))
                    sync()
                    for s in range(4):
                        sg = ti * 4 + s
                        if sg >= 2:
                            e.wait_ge(sm_tr, trA_sub(sg - 2) + 1)
                        e.tensor_scalar(sb_xntk[:, (s % 2) * C:(s % 2 + 1) * C],
                                        sb_xa[:, (sg % 5) * C:(sg % 5 + 1) * C],
                                        sb_st[:, base + 4 + s:base + 5 + s],
                                        sb_st[:, base + 20 + s:base + 21 + s],
                                        ALU.add, ALU.mult).then_inc(sm_xnt, 1)

                def emit_gate(pv):
                    base = 32 * (pv % 2)
                    e.wait_ge(sm_gp, pv + 1)
                    if pv > 0:
                        e.wait_ge(sm_gev, 6 * pv)
                    e.tensor_copy(sb_gs[:, 0:32], ps_gate[:]).then_inc(sm_gev, 1)
                    e.wait_ge(sm_gev, 6 * pv + 1)
                    e.tensor_add(sb_gs[:, 0:32], sb_gs[:, 0:32], sb_bgb[:]).then_inc(sm_gev, 1)
                    for s in range(4):
                        e.wait_ge(sm_qe, 24 * pv + 20 + s + 1)
                        mark(e.reduce_sum(sb_st[:, base + 28 + s:base + 29 + s],
                                          sb_gs[:, 32 + s * 8:32 + (s + 1) * 8], axis=AX.X))
                        sync()
                        e.reciprocal(sb_st[:, base + 24 + s:base + 25 + s],
                                     sb_st[:, base + 28 + s:base + 29 + s])
                        mark(e.reduce_max(sb_st[:, base + 28 + s:base + 29 + s],
                                          sb_gs[:, 32 + s * 8:32 + (s + 1) * 8], axis=AX.X))
                        sync()
                        mark(e.tensor_scalar(sb_gs[:, s * 8:(s + 1) * 8],
                                             sb_gs[:, 32 + s * 8:32 + (s + 1) * 8],
                                             sb_st[:, base + 28 + s:base + 29 + s], None, ALU.is_ge))
                        sync()
                        e.scalar_tensor_tensor(sb_g[:, (pv * 4 + s) * 8:(pv * 4 + s + 1) * 8],
                                               sb_gs[:, 32 + s * 8:32 + (s + 1) * 8],
                                               sb_st[:, base + 24 + s:base + 25 + s],
                                               sb_gs[:, s * 8:(s + 1) * 8],
                                               ALU.mult, ALU.mult).then_inc(sm_gev, 1)

                for b in range(B):
                    for tt in range(NTILE_SEQ):
                        ti = b * NTILE_SEQ + tt
                        emit_stats(ti, b, tt)
                        if tt > 0:
                            emit_trc(ti - 1, b)
                        emit_xn(ti, b)
                        if tt > 0:
                            emit_gate(ti - 1)
                    emit_trc(b * NTILE_SEQ + 3, b)
                    emit_gate(b * NTILE_SEQ + 3)
                    for qt in range(NQT):
                        qg = NQT * b + qt
                        n = nkb(qt)
                        if qt == 0:
                            e.wait_ge(sm_xb, 16 * (NQT * b + b + 1))
                        e.wait_ge(sm_s, sgi(b, qt, n - 1) + 1)
                        mark(e.tensor_add(ps_s[:, (n - 1) * 512:n * 512],
                                          ps_s[:, (n - 1) * 512:n * 512],
                                          sb_msk[:, (qt % 4) * 512:(qt % 4 + 1) * 512]))
                        sync()
                        mark(e.reduce_max(sb_rs[:, 0:1], ps_s[:, 0:n * 512], axis=AX.X, negate=True))
                        sync()
                        e.tensor_scalar_mul(sb_rs[:, 0:1], sb_rs[:, 0:1], SCALE).then_inc(sm_max, 1)
                        for kb in range(n):
                            g = trB(b, qt, kb)
                            e.wait_ge(sm_tr, g + 1)
                            if kb == 0 and qg >= 1:
                                e.wait_ge(sm_av, qg)
                            e.tensor_copy(sb_pT[:, kb * 512:(kb + 1) * 512],
                                          ps_tr[:, (g % 2) * 1024:(g % 2 + 1) * 1024][:, 0:512]
                                          ).then_inc(sm_trc, 1)
                        e.wait_ge(sm_exp, qg + 1)
                        mark(e.reduce_sum(sb_rs[:, 1:2],
                                          sb_p[:, (qg % 2) * 2048:(qg % 2) * 2048 + n * 512],
                                          axis=AX.X))
                        sync()
                        mark(e.reciprocal(sb_rs[:, 2:3], sb_rs[:, 1:2]))
                        sync()
                        e.wait_ge(sm_av, qg + 1)
                        e.wait_ge(sm_xb, 16 * (qg + 1) + 16 * (b + 1))
                        if qg >= 2:
                            e.wait_ge(sm_y0, Y0M * 16 * (qg - 1))
                        e.scalar_tensor_tensor(sb_y0[:, (qg % 2) * C:(qg % 2 + 1) * C],
                                               ps_av[:], sb_rs[:, 2:3],
                                               sb_xb2[:, (qg % 2) * C:(qg % 2 + 1) * C],
                                               ALU.mult, ALU.add).then_inc(sm_stt, 1)

            @blk.tensor
            def _(e):
                e.wait_ge(sm_pro, 33)
                for b in range(B):
                    for tt in range(NTILE_SEQ):
                        ti = b * NTILE_SEQ + tt
                        for s in range(4):
                            g = trA(b, tt, s)
                            e.wait_ge(sm_xnt, ti * 4 + s + 1)
                            if g >= 2:
                                e.wait_ge(sm_trc, g - 1)
                            for c in range(NCC):
                                tr = e.transpose(
                                    ps_tr[:, (g % 2) * 1024 + c * 128:(g % 2) * 1024 + (c + 1) * 128],
                                    sb_xntk[:, (s % 2) * C + c * 128:(s % 2) * C + (c + 1) * 128],
                                    sb_id[:])
                                if c == NCC - 1:
                                    tr.then_inc(sm_tr, 1)
                        e.wait_ge(sm_trc, trA(b, tt, 3) + 1)
                        if b == 0 and tt == 0:
                            e.wait_ge(sm_wqkv, 64)
                        if b > 0 and tt == 0:
                            e.wait_ge(sm_wqkv, 64 + 48 * b)
                            e.wait_ge(sm_exp, NQT * b)
                        xf = sb_xnf[:, (ti % 2) * NCC * TT:(ti % 2 + 1) * NCC * TT]
                        jj = 0
                        for wi, wbuf in enumerate([wqkv_a[0], wqkv_a[1]]):
                            for m in range(8):
                                j = jj
                                jj += 1
                                e.wait_ge(sm_qe, 24 * ti + j - 1) if (24 * ti + j) >= 2 else None
                                pb = ps_qkv[:, (j % 2) * 512:(j % 2 + 1) * 512]
                                for c in range(NCC):
                                    mm = e.matmul(pb, wbuf[:, c * C + m * 128:c * C + (m + 1) * 128],
                                                  xf[:, c * TT:(c + 1) * TT],
                                                  start=(c == 0), stop=(c == NCC - 1))
                                    if c == NCC - 1:
                                        mm.then_inc(sm_mm, 1)
                        for s in range(4):
                            j = 16 + s
                            e.wait_ge(sm_qe, 24 * ti + j - 1 + 1) if (ti > 0 or s > 0) else None
                            for c in range(NCC):
                                for hh in range(2):
                                    mm = e.matmul(ps_v[:, hh * 512:(hh + 1) * 512],
                                                  xf[:, c * TT + s * 128:c * TT + (s + 1) * 128],
                                                  wqkv_a[2][:, c * C + hh * 512:c * C + (hh + 1) * 512],
                                                  start=(c == 0), stop=(c == NCC - 1))
                                    if c == NCC - 1 and hh == 1:
                                        mm.then_inc(sm_mm, 1)
                        if ti > 0:
                            e.wait_ge(sm_gev, 6 * (ti - 1) + 1)
                        if b > 0 and tt == 0:
                            e.wait_ge(sm_stt, NQT * b)
                        for s in range(4):
                            for c in range(NCC):
                                mm = e.matmul(ps_gate[:, s * 8:(s + 1) * 8],
                                              xf[:, c * TT + s * 128:c * TT + (s + 1) * 128],
                                              sb_wg[:, c * 8:(c + 1) * 8],
                                              start=(s == 0 and c == 0), stop=(c == NCC - 1))
                                if s == 3 and c == NCC - 1:
                                    mm.then_inc(sm_gp, 1)
                    for qt in range(NQT):
                        qg = NQT * b + qt
                        n = nkb(qt)
                        for kb in range(n):
                            if qg > 0 and kb == 0:
                                e.wait_ge(sm_exp, qg)
                            if qt == 0 and kb == 0:
                                e.wait_ge(sm_qe, 96 * (b + 1))
                            for c in range(8):
                                mm = e.matmul(ps_s[:, kb * 512:(kb + 1) * 512],
                                              sb_q[:, c * T + qt * 128:c * T + (qt + 1) * 128],
                                              sb_k[:, c * T + kb * 512:c * T + (kb + 1) * 512],
                                              start=(c == 0), stop=(c == 7))
                                if c == 7:
                                    mm.then_inc(sm_s, 1)
                        e.wait_ge(sm_exp, qg + 1)
                        for kb in range(n):
                            g = trB(b, qt, kb)
                            if g >= 2:
                                e.wait_ge(sm_trc, g - 1)
                            for u in range(4):
                                tr = e.transpose(
                                    ps_tr[:, (g % 2) * 1024 + u * 128:(g % 2) * 1024 + (u + 1) * 128],
                                    sb_p[:, (qg % 2) * 2048 + kb * 512 + u * 128:
                                         (qg % 2) * 2048 + kb * 512 + (u + 1) * 128],
                                    sb_id[:])
                                if u == 3:
                                    tr.then_inc(sm_tr, 1)
                        e.wait_ge(sm_trc, trB(b, qt, n - 1) + 1)
                        if qg > 0:
                            e.wait_ge(sm_stt, qg)
                        if qt == 0:
                            e.wait_ge(sm_gev, 6 * (4 * b + 3) + 1)
                        for sc in range(n * 4):
                            for hh in range(2):
                                mm = e.matmul(ps_av[:, hh * 512:(hh + 1) * 512],
                                              sb_pT[:, sc * 128:(sc + 1) * 128],
                                              sb_v[:, sc * C + hh * 512:sc * C + (hh + 1) * 512],
                                              start=(sc == 0), stop=(sc == n * 4 - 1))
                                if sc == n * 4 - 1 and hh == 1:
                                    mm.then_inc(sm_av, 1)

        es_c = contextlib.ExitStack()
        with es_c:
            EC = es_c.enter_context
            sb_w1 = EC(nc.sbuf_tensor("sb_w1", [128, NCC * DH], BF16))
            sb_w2 = EC(nc.sbuf_tensor("sb_w2", [128, NHC * C], BF16))
            sb_xnc = EC(nc.sbuf_tensor("sb_xnc", [128, 2 * NCC * TT], BF16))
            sb_h = EC(nc.sbuf_tensor("sb_h", [128, NHC * TT], BF16))
            sb_y = EC(nc.sbuf_tensor("sb_y", [128, 2 * 2 * C], F32))
            sb_b2 = EC(nc.sbuf_tensor("sb_b2", [1, C], BF16))
            @blk.sync
            def _(e):
                e.wait_ge(sm_exp, NQT * B)
                e.wait_ge(sm_stt, NQT * B)
                e.wait_ge(sm_av, NQT * B)
                e.wait_ge(sm_y0, Y0M * 16 * NQT * B)
                for ex in range(NE):
                    for tt in range(NTILE):
                        i = ex * NTILE + tt
                        if i >= 2:
                            e.wait_ge(sm_h, NHC * (i - 1))
                        tok0 = tt * TT
                        src = xnT[:, tok0:tok0 + TT].rearrange("(g p) t -> p g t", p=128)
                        dst = sb_xnc[:, (i % 2) * NCC * TT:(i % 2 + 1) * NCC * TT]
                        e.dma_start(out=dst.rearrange("p (g t) -> p g t", g=NCC),
                                    in_=src).then_inc(sm_xn, 16)
                if debug:
                    e.wait_ge(sm_xnfs, 16 * NTILE)
                    e.dma_start(out=xnT_dbg, in_=xnT).then_inc(sm_xn, 16)
                    e.dma_start(out=g_dbg, in_=sb_g[:]).then_inc(sm_xn, 16)

            @blk.tensor
            def _(e):
                for ex in range(NE):
                    for tt in range(NTILE):
                        i = ex * NTILE + tt
                        e.wait_ge(sm_xn, 16 * (i + 1))
                        xt = sb_xnc[:, (i % 2) * NCC * TT:(i % 2 + 1) * NCC * TT]
                        for k in range(NHC):
                            gi = i * NHC + k
                            if tt == 0 and k == 0:
                                e.wait_ge(sm_w, 80 * ex + 16)
                            if tt == 0 and k == NHC // 2:
                                e.wait_ge(sm_w, 80 * ex + 32)
                            if gi >= 4:
                                e.wait_ge(sm_silu, gi - 3)
                            pb = ps_h[:, (gi % 4) * TT:(gi % 4 + 1) * TT]
                            for c in range(NCC):
                                mm = e.matmul(pb, sb_w1[:, (c * DH + k * 128):(c * DH + (k + 1) * 128)],
                                              xt[:, c * TT:(c + 1) * TT],
                                              start=(c == 0), stop=(c == NCC - 1))
                                if c == NCC - 1:
                                    mm.then_inc(sm_h, 1)
                        if tt == 0:
                            e.wait_ge(sm_w, 80 * (ex + 1))
                        for s in range(4):
                            oi = i * 4 + s
                            if s == 1:
                                e.wait_ge(sm_silu, (i + 1) * NHC)
                            if oi >= 2:
                                e.wait_ge(sm_y, oi - 1)
                            po = ps_o[:, (oi % 2) * C:(oi % 2 + 1) * C]
                            for k in range(NHC):
                                if s == 0:
                                    e.wait_ge(sm_silu, i * NHC + k + 1)
                                hT = sb_h[:, (k * TT + s * 128):(k * TT + (s + 1) * 128)]
                                e.matmul(po[:, 0:512], hT, sb_w2[:, k * C:k * C + 512],
                                         start=(k == 0), stop=False)
                                e.matmul(po[:, 512:1024], hT, sb_w2[:, k * C + 512:(k + 1) * C],
                                         start=(k == 0), stop=False)
                            e.matmul(po[:, 0:512], ones[0:1, 0:128], sb_b2[0:1, 0:512],
                                     start=False, stop=True)
                            e.matmul(po[:, 512:1024], ones[0:1, 0:128], sb_b2[0:1, 512:1024],
                                     start=False, stop=True).then_inc(sm_o, 1)

            @blk.scalar
            def _(e):
                for ex in range(NE):
                    for tt in range(NTILE):
                        i = ex * NTILE + tt
                        for k in range(NHC):
                            gi = i * NHC + k
                            e.wait_ge(sm_h, gi + 1)
                            e.activation(sb_h[:, k * TT:(k + 1) * TT],
                                         ps_h[:, (gi % 4) * TT:(gi % 4 + 1) * TT],
                                         AF.Silu, bias=sb_b1[:, ex * NHC + k:ex * NHC + k + 1]
                                         ).then_inc(sm_silu, 1)

            @blk.vector
            def _(e):
                for ex in range(NE):
                    for tt in range(NTILE):
                        i = ex * NTILE + tt
                        for s in range(4):
                            oi = i * 4 + s
                            e.wait_ge(sm_o, oi + 1)
                            if oi >= 4:
                                e.wait_ge(sm_acc, 16 * (oi // 2 - 1))
                            gcol = sb_g[:, (tt * 4 + s) * NE + ex:(tt * 4 + s) * NE + ex + 1]
                            j = oi // 2
                            e.tensor_scalar(sb_y[:, (j % 2) * 2 * C + (s % 2) * C:
                                                 (j % 2) * 2 * C + (s % 2 + 1) * C],
                                            ps_o[:, (oi % 2) * C:(oi % 2 + 1) * C], gcol, None,
                                            ALU.mult).then_inc(sm_y, 1)

            @blk.gpsimd
            def _(e):
                def load_w(ex):
                    if ex > 0:
                        e.wait_ge(sm_h, NHC * NTILE * ex)
                    for half in range(2):
                        e.dma_start(
                            out=sb_w1[:].rearrange("p (g h) -> p g h", g=NCC)[:, :, half * 2048:(half + 1) * 2048],
                            in_=w1_in[ex].rearrange("(g p) h -> p g h", p=128)[:, :, half * 2048:(half + 1) * 2048]
                        ).then_inc(sm_w, 16)

                def load_w2(ex):
                    if ex > 0:
                        e.wait_ge(sm_o, NTILE * 4 * ex)
                    for half in range(2):
                        e.dma_start(
                            out=sb_w2[:].rearrange("p (g d) -> p g d", g=NHC)[:, half * 16:(half + 1) * 16, :],
                            in_=w2_in[ex].rearrange("(g p) d -> p g d", p=128)[:, half * 16:(half + 1) * 16, :]
                        ).then_inc(sm_w, 16)
                    e.dma_start(out=sb_b2[:], in_=b2_in[ex].unsqueeze(0)).then_inc(sm_w, 16)

                e.wait_ge(sm_exp, NQT * B)
                e.wait_ge(sm_stt, NQT * B)
                e.wait_ge(sm_av, NQT * B)
                e.wait_ge(sm_y0, Y0M * 16 * NQT * B)
                load_w(0)
                load_w2(0)
                for ex in range(NE):
                    for tt in range(NTILE):
                        for half in range(2):
                            a = (ex * NTILE + tt) * 2 + half
                            e.wait_ge(sm_y, a * 2 + 2)
                            b = tt // NTILE_SEQ
                            r0 = (tt % NTILE_SEQ) * TT + half * 256
                            dst = y_out[b, r0:r0 + 256, :]
                            e.dma_start(out=dst.rearrange("(s p) d -> p s d", p=128),
                                        in_=sb_y[:, (a % 2) * 2 * C:(a % 2 + 1) * 2 * C]
                                        .rearrange("p (s d) -> p s d", s=2),
                                        accum_op=ALU.add).then_inc(sm_acc, 16)
                        if tt == 14 and ex < NE - 1:
                            load_w(ex + 1)
                    if ex < NE - 1:
                        load_w2(ex + 1)
                e.wait_ge(sm_acc, 16 * NE * NTILE * 2)

    return nc


def _host_inputs(shard, ins):
    bf = ml_dtypes.bfloat16
    b1 = np.asarray(ins["L0_ffn_b1"], dtype=np.float32)
    m = {
        "x": np.ascontiguousarray(shard),
        "wq": np.ascontiguousarray(np.asarray(ins["L0_attn_Wq"], dtype=np.float32)),
        "wk": np.ascontiguousarray(np.asarray(ins["L0_attn_Wk"], dtype=np.float32)),
        "wv": np.ascontiguousarray(np.asarray(ins["L0_attn_Wv"], dtype=np.float32)),
        "wg": np.ascontiguousarray(np.asarray(ins["L0_ffn_Wg"], dtype=np.float32)),
        "bgb": np.ascontiguousarray(np.tile(np.asarray(ins["L0_ffn_bg"], dtype=np.float32), (128, 4))),
        "w1": np.ascontiguousarray(np.asarray(ins["L0_ffn_W1"], dtype=np.float32)),
        "b1T": np.ascontiguousarray(b1.reshape(NE, NHC, 128).transpose(0, 2, 1)),
        "w2": np.ascontiguousarray(np.asarray(ins["L0_ffn_W2"], dtype=np.float32)),
        "b2": np.ascontiguousarray(np.asarray(ins["L0_ffn_b2"], dtype=np.float32)),
        "ident": np.eye(128, dtype=bf),
    }
    p = np.arange(128)[:, None]
    col = np.arange(512)[None, :]
    mask = np.zeros((128, 4 * 512), dtype=np.float32)
    for r in range(4):
        mask[:, r * 512:(r + 1) * 512] = np.where(col <= r * 128 + p, 0.0, NEG)
    m["mask"] = mask
    return m


def kernel(**inputs):
    x = np.asarray(inputs["L0_attn_x"], dtype=np.float32)
    shards = np.split(np.ascontiguousarray(x), 8, axis=0)
    nc = build_nc()
    in_maps = [_host_inputs(s, inputs) for s in shards]
    res = run_bass_kernel_spmd(nc, in_maps, core_ids=list(range(8)))
    return np.concatenate([np.asarray(r["y"]) for r in res.results], axis=0)
